# version 13
# baseline (speedup 1.0000x reference)
import math
import numpy as np
import concourse.bass as bass
import concourse.mybir as mybir
from concourse.bass_utils import run_bass_kernel_spmd

F32 = mybir.dt.float32
F32R = mybir.dt.float32r
FP8 = mybir.dt.float8e4
ALU = mybir.AluOpType
ACTF = mybir.ActivationFunctionType
AXX = mybir.AxisListType.X

NCORES = 8
S, D = 8192, 2048
SQ = S // NCORES
DC = D // 128
NH = 16
QT = 512
NQT = 2
EPS = 1e-5
FP8M = 448.0 * 0.8
ISQ = 1.0 / math.sqrt(128.0)
MI, MH, MD = 4096, 64, 64
NG, NS = 8, 128
GN = NG * NS
CONV = MI + 2 * GN
NCC = CONV // 128
LC = 128
NTC = SQ // LC

N_SUBLAYERS = 4
DEBUG_DUMPS = False

ENGINES = ("sync", "gpsimd", "scalar", "vector", "tensor")


class Prog:

    def __init__(self, nc):
        self.nc = nc
        self.q = {e: [] for e in ENGINES}
        self.esem = {}
        self.ecnt = {}
        self.tag_eng = {}
        self.tag_hist = {}
        self.last_wait = {e: {} for e in ENGINES}

    def _key(self, eng, is_dma):
        k = f"{eng}_dma" if is_dma else eng
        if k not in self.esem:
            self.esem[k] = self.nc.alloc_semaphore(f"prog_{k}")
            self.ecnt[k] = 0
        return k

    def v(self, name):
        h = self.tag_hist.get(name)
        return len(h) if h else 0

    def _record(self, eng, name, inc, is_dma):
        k = self._key(eng, is_dma)
        assert self.tag_eng.setdefault(name, k) == k, name
        self.ecnt[k] += inc
        self.tag_hist.setdefault(name, []).append(self.ecnt[k])
        return len(self.tag_hist[name])

    def _waits(self, eng, wait):
        out = {}
        for s, val in wait:
            if val <= 0 or s not in self.tag_hist:
                continue
            hist = self.tag_hist[s]
            assert val <= len(hist), (s, val, len(hist))
            peng = self.tag_eng[s]
            a = hist[val - 1]
            if out.get(peng, -1) < a:
                out[peng] = a
        res = []
        for peng, a in out.items():
            if self.last_wait[eng].get(peng, -1) < a:
                self.last_wait[eng][peng] = a
                res.append((self.esem[peng], a))
        return res

    def op(self, eng, builder, sem=None, inc=1, wait=(), is_dma=False):
        ws = self._waits(eng, wait)
        h = None
        if sem is not None:
            self._record(eng, sem, inc, is_dma)
            h = self.esem[self._key(eng, is_dma)]

        def fn(E, builder=builder, ws=ws, h=h, inc=inc):
            for sh, val in ws:
                E.wait_ge(sh, val)
            inst = builder(E)
            if h is not None:
                inst.then_inc(h, inc)
        self.q[eng].append(fn)
        return self.v(sem) if sem else None

    def dma(self, eng, out, in_, sem=None, wait=()):
        if sem is None:
            sem = f"_dma_{eng}"
        return self.op(eng, lambda E, o=out, i=in_: E.dma_start(out=o, in_=i),
                       sem=sem, inc=16, wait=wait, is_dma=True)

    def rdma(self, eng, out, in_, sem=None, wait=()):
        return self.dma(eng, out.bitcast(F32R) if out.dtype != F32R else out,
                        in_.bitcast(F32R) if in_.dtype != F32R else in_,
                        sem=sem, wait=wait)

    def wait(self, eng, wait):
        ws = self._waits(eng, wait)
        if not ws:
            return

        def fn(E, ws=ws):
            for sh, val in ws:
                E.wait_ge(sh, val)
        self.q[eng].append(fn)

    def collective(self, kind, op, ins, outs, sem, wait=()):
        ws = self._waits("gpsimd", wait)
        self._record("gpsimd", sem, 1, False)
        h = self.esem[self._key("gpsimd", False)]

        def fn(E, ws=ws, h=h, kind=kind, op=op, ins=ins, outs=outs):
            for sh, val in ws:
                E.wait_ge(sh, val)
            E.collective_compute(kind, op, replica_groups=[list(range(NCORES))],
                                 ins=ins, outs=outs).then_inc(h, 1)
        self.q["gpsimd"].append(fn)
        return self.v(sem)


def build_program():
    nc = bass.Bass("TRN2", target_bir_lowering=False, dynamic_dma_scratch_size=1024,
                   detect_race_conditions=False)
    nc.allow_low_precision(reason="float32r outputs are 4-byte fp32 storage").__enter__()
    p = Prog(nc)

    def inp(name, shape):
        return nc.dram_tensor(name, list(shape), F32, kind="ExternalInput").ap()

    xin = inp("xT", (D, SQ))
    wq = inp("wq", (DC * 4, 128, 512))
    wk = inp("wk", (DC * 4, 128, 512))
    wv = inp("wv", (2, DC, 128, 1024))
    wo = inp("wo", (DC * 4, 128, 512))
    ln_w = inp("ln_w", (128, DC))
    ln_b = inp("ln_b", (128, DC))
    ffnw = inp("ffnw", (DC * 4, 128, 512))
    ffnb = inp("ffnb", (128, DC))
    wz = inp("wz", (32 * 4, 128, 512))
    wxp = inp("wxp", (32 * 4, 128, 512))
    wbp = inp("wbp", (8 * 4, 128, 512))
    wcp = inp("wcp", (8 * 4, 128, 512))
    wdt = inp("wdt", (2, 128, 512))
    wom = inp("wom", (DC * 8, 128, 512))
    convw = inp("convw", (128, NCC, 4))
    convb = inp("convb", (128, NCC))
    dtbias = inp("dtbias", (64, 1))
    alog = inp("alog", (64, 1))
    dvec = inp("dvec", (64, 1))
    normw = inp("normw", (128, MI // 128))
    ropec = inp("ropec", (64, SQ))
    ropesn = inp("ropes", (64, SQ))
    prevmask = inp("prevmask", (1, 8))
    halosel = inp("halosel", (1, 8))
    haloselB = inp("haloselB", (128, 8))
    identc = inp("identc", (128, 128))
    trimaskc = inp("trimask", (128, 128))
    onesmat = inp("onesmat", (128, 128))
    onescol = inp("onescol", (128, 1))
    epscol = inp("epscol", (128, 1))
    zrow = inp("zrow", (1, MH * MD))

    yout = nc.dram_tensor("y", [D, SQ], F32, kind="ExternalOutput").ap()

    kvb = nc.dram_tensor("kvb", [4097, SQ], F32)
    kvag = nc.dram_tensor("kvag", [NCORES * 4097, SQ], F32)
    kvcp = (nc.dram_tensor("kvcp", [NCORES * 4097, SQ], F32, kind="ExternalOutput")
            if DEBUG_DUMPS else None)
    qcp = (nc.dram_tensor("qcp", [128, 16384], F32, kind="ExternalOutput")
           if DEBUG_DUMPS else None)
    amxb = [nc.dram_tensor(f"amxb{i}", [1, 8], F32) for i in range(2)]
    amxo = [nc.dram_tensor(f"amxo{i}", [1, 8], F32) for i in range(2)]
    xspill = nc.dram_tensor("xspill", [128, DC * SQ], F32)
    bspill = nc.dram_tensor("bspill", [NG, 128, SQ], F32)
    btspill = nc.dram_tensor("btspill", [NG, NTC, 128, 128], F32)
    cspill = nc.dram_tensor("cspill", [NG, 128, SQ], F32)
    halob = nc.dram_tensor("halob", [128, DC * 3], F32)
    haloag = nc.dram_tensor("haloag", [NCORES * 128, DC * 3], F32)
    stb = nc.dram_tensor("stb", [NS + 1, MH * MD], F32)
    dkind = "ExternalOutput" if DEBUG_DUMPS else "Internal"
    dbg1 = nc.dram_tensor("dbg1", [128, 1536], F32, kind=dkind)
    dbg2 = nc.dram_tensor("dbg2", [128, 2048], F32, kind=dkind)
    dbg3 = nc.dram_tensor("dbg3", [128, DC * SQ], F32, kind=dkind)
    stag = nc.dram_tensor("stag", [NCORES * (NS + 1), MH * MD], F32)

    XR = nc.alloc_sbuf_tensor("XR", [128, DC * SQ], F32R)
    A1 = nc.alloc_sbuf_tensor("A1", [128, DC * SQ], F32R)
    A2 = nc.alloc_sbuf_tensor("A2", [128, DC * SQ], F32R)
    EX = nc.alloc_sbuf_tensor("EX", [128, 2048], F32R)
    WSB = nc.alloc_sbuf_tensor("WSB", [128, 1024], F32R)
    BCT = nc.alloc_sbuf_tensor("BCT", [128, 2048], F32)
    RT = nc.alloc_sbuf_tensor("RT", [128, 1280], F32R)
    CT = nc.alloc_sbuf_tensor("CT", [128, 384], F32R)
    SM = nc.alloc_sbuf_tensor("SM", [128, 328], F32)
    DBv = nc.alloc_sbuf_tensor("DBv", [128, 64], F32)
    CBt = nc.alloc_sbuf_tensor("CBt", [128, 16], F32)
    S1 = nc.alloc_sbuf_tensor("S1", [64, 8], F32R)
    R1 = nc.alloc_sbuf_tensor("R1", [1, 64], F32R)
    OCl = nc.alloc_sbuf_tensor("OCl", [128, 1], F32R)
    EPC = nc.alloc_sbuf_tensor("EPC", [128, 1], F32)
    QS = nc.alloc_sbuf_tensor("QS", [128, 1024], FP8)
    WS8 = nc.alloc_sbuf_tensor("WS8", [128, 512], FP8)
    HSL = nc.alloc_sbuf_tensor("HSL", [128, 8], F32)

    ident = CT[:, 0:128]
    tri = CT[:, 128:256]
    onesm = CT[:, 256:384]
    OC = OCl[:]

    SM_LNW, SM_LNB, SM_FFB, SM_NRMW, SM_CVB, SM_CVW, SM_SCL = 0, 16, 32, 48, 80, 128, 320

    PS = [nc.alloc_psum_tensor(f"ps{i}", [128, 512], F32) for i in range(8)]

    p.dma("sync", stb.ap()[NS:NS + 1, :], zrow, sem="xl")
    p.dma("sync", XR[:].rearrange("p (c t) -> p c t", t=SQ),
          xin.bitcast(F32R).rearrange("(c p) t -> p c t", p=128), sem="xl")
    for dst, src in ((ident, identc), (tri, trimaskc), (onesm, onesmat), (OC, onescol)):
        p.dma("sync", dst, src.bitcast(F32R), sem="xl")
    p.dma("sync", EPC[:], epscol, sem="xl")

    p.dma("sync", HSL[:], haloselB, sem="xl")
    p.dma("sync", SM[:, SM_LNW:SM_LNW + 16], ln_w, sem="xl")
    p.dma("sync", SM[:, SM_LNB:SM_LNB + 16], ln_b, sem="xl")
    p.dma("sync", SM[:, SM_FFB:SM_FFB + 16], ffnb, sem="xl")
    p.dma("sync", SM[:, SM_NRMW:SM_NRMW + 32], normw, sem="xl")
    p.dma("sync", SM[:, SM_CVB:SM_CVB + 48], convb, sem="xl")
    p.dma("sync", SM[:, SM_CVW:SM_CVW + 192].rearrange("p (c k) -> p c k", k=4),
          convw, sem="xl")
    p.dma("sync", S1[:, 0:1], dtbias.bitcast(F32R), sem="xl")
    p.dma("sync", S1[:, 1:2], alog.bitcast(F32R), sem="xl")
    p.dma("sync", S1[:, 2:3], dvec.bitcast(F32R), sem="xl")
    p.dma("sync", R1[0:1, 48:56], prevmask.bitcast(F32R), sem="xl")
    p.dma("sync", R1[0:1, 56:64], halosel.bitcast(F32R), sem="xl")
    XL = p.v("xl")

    def mmrow(E, bank, row_ap, n0, w):
        ap = row_ap[0:1, n0:n0 + w]
        if ap.dtype != F32:
            ap = ap.bitcast(F32)
        return E.matmul(bank[:, 0:w], onesm[0:1, 0:128].bitcast(F32),
                        ap, start=True, stop=True)

    aB = BCT[:, 0:1024]
    bB = BCT[:, 1024:2048]

    def ln_phase(sqdst, double, wait_extra=()):
        we = list(wait_extra)
        for c in range(DC):
            p.op("scalar", lambda E, c=c: E.activation(
                out=sqdst[:, c * SQ:(c + 1) * SQ], in_=XR[:, c * SQ:(c + 1) * SQ],
                func=ACTF.Square), sem="lsq",
                wait=(we + [("xl", XL)]) if c == 0 else ())
        sqv = p.v("lsq")

        def stats(E):
            last = None
            for c in range(DC):
                for ti in range(NQT):
                    sl = slice(ti * QT, (ti + 1) * QT)
                    last = E.matmul(PS[ti][0:1, :], OC,
                                    XR[:, c * SQ:(c + 1) * SQ][:, sl],
                                    start=(c == 0), stop=(c == DC - 1))
                    last = E.matmul(PS[2 + ti][0:1, :], OC,
                                    sqdst[:, c * SQ:(c + 1) * SQ][:, sl],
                                    start=(c == 0), stop=(c == DC - 1))
            return last
        p.op("tensor", stats, sem="lst", wait=we + [("lsq", sqv), ("xl", XL)])
        lstv = p.v("lst")
        mrow = EX[0:1, 0:1024]
        rv = BCT[0:1, 1024:2048]
        rt_ = BCT[0:1, 0:1024]
        RXR = RT[0:1, 0:1024]

        def rows1(E):
            last = None
            for ti in range(NQT):
                sl = slice(ti * QT, (ti + 1) * QT)
                E.tensor_scalar(out=mrow[0:1, sl], in0=PS[ti][0:1, :], scalar1=1.0 / D,
                                scalar2=None, op0=ALU.mult)
                E.tensor_scalar(out=rv[0:1, sl], in0=PS[2 + ti][0:1, :],
                                scalar1=1.0 / D, scalar2=None, op0=ALU.mult)
                E.tensor_tensor(out=rt_[0:1, sl], in0=mrow[0:1, sl].bitcast(F32),
                                in1=mrow[0:1, sl].bitcast(F32), op=ALU.mult)
                last = E.tensor_tensor(out=rv[0:1, sl], in0=rv[0:1, sl],
                                       in1=rt_[0:1, sl], op=ALU.subtract)
            return last
        p.op("vector", rows1, sem="lrm", wait=[("lst", lstv)])
        if double:
            p.op("vector", lambda E: E.tensor_copy(out=rt_, in_=rv),
                 sem="lvc", wait=[("lrm", p.v("lrm"))])
        p.op("scalar", lambda E: E.activation(out=rv, in_=rv, func=ACTF.Sqrt,
                                              bias=EPC[0:1, 0:1]),
             sem="lr0", wait=[("lvc", p.v("lvc")), ("lrm", p.v("lrm"))])
        p.op("vector", lambda E: E.reciprocal(out=rv, in_=rv),
             sem="lr1", wait=[("lr0", p.v("lr0"))])
        if double:
            def v2(E):
                E.tensor_tensor(out=rt_, in0=rt_, in1=rv, op=ALU.mult)
                return E.tensor_tensor(out=rt_, in0=rt_, in1=rv, op=ALU.mult)
            p.op("vector", v2, sem="lv2", wait=[("lr1", p.v("lr1"))])
            p.op("scalar", lambda E: E.activation(out=rt_, in_=rt_, func=ACTF.Sqrt,
                                                  bias=EPC[0:1, 0:1]),
                 sem="lq2", wait=[("lv2", p.v("lv2"))])
            p.op("vector", lambda E: E.reciprocal(out=rt_, in_=rt_),
                 sem="lr2", wait=[("lq2", p.v("lq2"))])
            p.op("vector", lambda E: E.tensor_tensor(out=rv, in0=rv, in1=rt_,
                                                     op=ALU.mult),
                 sem="lar", wait=[("lr2", p.v("lr2"))])
        else:
            p.op("vector", lambda E: E.tensor_copy(out=rv[0:1, 0:4], in_=rv[0:1, 0:4]),
                 sem="lar", wait=[("lr1", p.v("lr1"))])
        p.op("vector", lambda E: E.tensor_copy(out=RXR, in_=rv), sem="lax",
             wait=[("lar", p.v("lar"))])

        def bca(E):
            mmrow(E, PS[4], RXR, 0, 512)
            return mmrow(E, PS[5], RXR, 512, 512)
        p.op("tensor", bca, sem="lba", wait=[("lax", p.v("lax"))])

        def cpa(E):
            E.tensor_copy(out=aB[:, 0:512], in_=PS[4][:, :])
            return E.tensor_copy(out=aB[:, 512:1024], in_=PS[5][:, :])
        p.op("vector", cpa, sem="lca", wait=[("lba", p.v("lba"))])

        def betarow(E):
            E.tensor_tensor(out=rv, in0=rv, in1=mrow.bitcast(F32), op=ALU.mult)
            return E.tensor_copy(out=RXR, in_=rv)
        p.op("vector", betarow, sem="lbr", wait=[("lca", p.v("lca"))])

        def bcb(E):
            mmrow(E, PS[4], RXR, 0, 512)
            return mmrow(E, PS[5], RXR, 512, 512)
        p.op("tensor", bcb, sem="lbb", wait=[("lbr", p.v("lbr"))])

        def cpb(E):
            E.tensor_copy(out=bB[:, 0:512], in_=PS[4][:, :])
            return E.tensor_copy(out=bB[:, 512:1024], in_=PS[5][:, :])
        p.op("vector", cpb, sem="lnd", wait=[("lbb", p.v("lbb"))])
        return p.v("lnd")

    ATT_DONE = 0
    if N_SUBLAYERS >= 1:
        lnv = ln_phase(A2, double=True)
        for c in range(DC):
            mid = A2[:, c * SQ:(c + 1) * SQ]

            def v1(E, c=c, mid=mid):
                E.tensor_tensor(out=mid, in0=XR[:, c * SQ:(c + 1) * SQ],
                                in1=aB, op=ALU.mult)
                return E.tensor_tensor(out=mid, in0=mid.bitcast(F32), in1=bB,
                                       op=ALU.subtract)
            p.op("vector", v1, sem="xnv", wait=[("lnd", lnv)] if c == 0 else ())
            p.op("scalar", lambda E, c=c, mid=mid: E.activation(
                out=A1[:, c * SQ:(c + 1) * SQ], in_=mid.bitcast(F32), func=ACTF.Identity,
                bias=SM[:, SM_LNB + c:SM_LNB + c + 1],
                scale=SM[:, SM_LNW + c:SM_LNW + c + 1]),
                sem="xn", wait=[("xnv", p.v("xnv"))])
        XN = p.v("xn")

        vflat = kvb.ap().rearrange("r t -> (r t)")[2048 * SQ:4096 * SQ].rearrange(
            "(t f) -> t f", f=D)
        for half in range(2):
            p.rdma("sync", A2[:].rearrange("p (a b) -> p a b", b=1024),
                   wv[half].rearrange("a p b -> p a b"), sem="wvl",
                   wait=[("xn", XN)] if half == 0 else
                        [("vmm", p.v("vmm"))])
            WVL = p.v("wvl")
            for tc in range(NTC):
                for vt in range(2):
                    bank = PS[(tc * 2 + vt) % 2]

                    def vmm(E, tc=tc, vt=vt, bank=bank):
                        last = None
                        for dc in range(DC):
                            last = E.matmul(
                                bank[:, :],
                                A1[:, dc * SQ + tc * 128: dc * SQ + tc * 128 + 128],
                                A2[:, dc * 1024 + vt * 512: dc * 1024 + vt * 512 + 512],
                                start=(dc == 0), stop=(dc == DC - 1))
                        return last
                    p.op("tensor", vmm, sem="vmm",
                         wait=[("wvl", WVL), ("xn", XN), ("vev", p.v("vev") - 1)])
                    vslot = EX[:, (p.v("vmm") % 2) * 512:(p.v("vmm") % 2) * 512 + 512]
                    p.op("scalar", lambda E, bank=bank, vslot=vslot: E.activation(
                        out=vslot, in_=bank[:, :], func=ACTF.Copy), sem="vev",
                        wait=[("vmm", p.v("vmm")), ("vst", p.v("vst") - 1)])
                    p.dma("scalar",
                          vflat[tc * 128:(tc + 1) * 128,
                                half * 1024 + vt * 512: half * 1024 + (vt + 1) * 512],
                          vslot.bitcast(F32), sem="vst", wait=[("vev", p.v("vev"))])
        VST = p.v("vst")
        VMMALL = p.v("vmm")

        for i in range(64):
            slot = WSB[:, (i % 2) * 512:(i % 2) * 512 + 512]
            p.rdma("sync", slot, ffnw[i], sem="fal",
                   wait=[("far", p.v("far") - 1)])

            def red(E, slot=slot, i=i):
                E.tensor_reduce(out=RT[:, 1025:1026],
                                in_=slot.bitcast(F32), axis=AXX, op=ALU.max,
                                apply_absolute_value=True)
                if i == 0:
                    return E.tensor_copy(out=RT[:, 1024:1025],
                                         in_=RT[:, 1025:1026].bitcast(F32))
                return E.tensor_tensor(out=RT[:, 1024:1025],
                                       in0=RT[:, 1024:1025].bitcast(F32),
                                       in1=RT[:, 1025:1026].bitcast(F32), op=ALU.max)
            p.op("vector", red, sem="far", wait=[("fal", p.v("fal"))])
        FARALL = p.v("far")
        p.op("tensor", lambda E: E.transpose(
            PS[6][0:1, 0:128].bitcast(F32R),
            RT[:, 1024:1025], ident),
            sem="fat", wait=[("far", FARALL), ("xl", XL)])
        p.op("vector", lambda E: E.tensor_reduce(
            out=R1[0:1, 1:2], in_=PS[6][0:1, 0:128], axis=AXX, op=ALU.max),
            sem="faw", wait=[("fat", p.v("fat"))])

        p.dma("sync", BCT[0:64, 0:SQ], ropec, sem="rtl", wait=[("xnv", p.v("xnv"))])
        p.dma("sync", BCT[0:64, SQ:2 * SQ], ropesn, sem="rtl")
        RTL = p.v("rtl")
        cosT = BCT[0:64, 0:SQ]
        sinT = BCT[0:64, SQ:2 * SQ]

        kst = EX[:, 0:1024]
        ksw = EX[:, 1024:2048]

        def rope_ops(tag, buf, wait_first=()):
            p.dma("scalar", ksw[0:32, 0:1024], buf[32:64, :],
                  sem=f"{tag}s", wait=[(f"{tag}r", p.v(f"{tag}r"))] + list(wait_first))
            p.dma("scalar", ksw[32:64, 0:1024], buf[0:32, :], sem=f"{tag}s")
            sv = p.v(f"{tag}s")

            def rp(E, buf=buf):
                E.tensor_tensor(out=ksw[0:64, 0:1024], in0=ksw[0:64, 0:1024].bitcast(F32),
                                in1=sinT, op=ALU.mult)
                E.tensor_tensor(out=buf[0:64, :], in0=buf[0:64, :].bitcast(F32),
                                in1=cosT, op=ALU.mult)
                return E.tensor_tensor(out=buf[0:64, :], in0=buf[0:64, :].bitcast(F32),
                                       in1=ksw[0:64, 0:1024].bitcast(F32), op=ALU.add)
            p.op("vector", rp, sem=f"{tag}r", wait=[(f"{tag}s", sv), ("rtl", RTL)])
            return p.v(f"{tag}r")

        for fc in range(DC):
            for g in range(4):
                i = fc * 4 + g
                slot = WSB[:, (i % 2) * 512:(i % 2) * 512 + 512]
                p.rdma("sync", slot, wk[i], sem="kl",
                       wait=[("kc", i - 1)] + ([("far", FARALL)] if i < 2 else []))
                lv = p.v("kl")

                def mmk(E, g=g, slot=slot):
                    last = None
                    for j in range(4):
                        dc = g * 4 + j
                        for ti in range(NQT):
                            last = E.matmul(
                                PS[4 + ti][:, :], slot[:, j * 128:(j + 1) * 128],
                                A1[:, dc * SQ + ti * QT: dc * SQ + (ti + 1) * QT],
                                start=(dc == 0), stop=(dc == DC - 1))
                    return last
                p.op("tensor", mmk, sem="kc",
                     wait=[("kl", lv), ("xn", XN)] +
                          ([("kev", p.v("kev"))] if g == 0 else []))
            def evk(E):
                E.activation(out=kst[:, 0:512], in_=PS[4][:, :], func=ACTF.Copy)
                return E.activation(out=kst[:, 512:1024], in_=PS[5][:, :],
                                    func=ACTF.Copy)
            p.op("scalar", evk, sem="kev",
                 wait=[("kc", p.v("kc")), ("kdma", p.v("kdma")),
                       ("ksq", p.v("ksq")), ("vst", VST)])
            rv = rope_ops("k", kst, [("kev", p.v("kev")),
                                     ("knm", p.v("knm"))])
            p.op("scalar", lambda E: E.activation(
                out=ksw[:, 0:1024], in_=kst.bitcast(F32), func=ACTF.Square),
                sem="ksq", wait=[("kr", rv)])

            def knmm(E):
                E.matmul(PS[6][0:1, :], OC, ksw[:, 0:512], start=True, stop=True)
                return E.matmul(PS[7][0:1, :], OC, ksw[:, 512:1024],
                                start=True, stop=True)
            p.op("tensor", knmm, sem="knm", wait=[("ksq", p.v("ksq")),
                                                  ("knr", p.v("knr")),
                                                  ("faw", p.v("faw"))])

            def knred(E, fc=fc):
                E.tensor_reduce(out=R1[0:1, 32 + fc:33 + fc], in_=PS[6][0:1, :],
                                axis=AXX, op=ALU.max)
                E.tensor_reduce(out=R1[0:1, 2:3], in_=PS[7][0:1, :], axis=AXX,
                                op=ALU.max)
                return E.tensor_tensor(out=R1[0:1, 32 + fc:33 + fc],
                                       in0=R1[0:1, 32 + fc:33 + fc],
                                       in1=R1[0:1, 2:3], op=ALU.max)
            p.op("vector", knred, sem="knr", wait=[("knm", p.v("knm"))])
            p.dma("scalar", kvb.ap()[fc * 128:(fc + 1) * 128, :], kst.bitcast(F32),
                  sem="kdma", wait=[("kr", rv)])
        p.dma("sync", RT[0:1, 0:1024], zrow[0:1, 0:1024].bitcast(F32R),
              sem="knz", wait=[("knm", p.v("knm"))])
        def nrow(E):
            return E.tensor_copy(out=RT[0:1, 0:16], in_=R1[0:1, 32:48].bitcast(F32))
        p.op("vector", nrow, sem="knw", wait=[("knr", p.v("knr")),
                                              ("knz", p.v("knz"))])
        p.dma("scalar", kvb.ap()[4096:4097, 0:1024], RT[0:1, 0:1024].bitcast(F32), sem="kdma",
              wait=[("knw", p.v("knw"))])
        KD = p.v("kdma")

        ag1 = p.collective("AllGather", ALU.bypass, [kvb.ap()], [kvag.ap()],
                           sem="cc", wait=[("kdma", KD), ("vst", VST)])

        def emit_qload(i):
            if i < 64:
                p.rdma("sync", WSB[:, (i % 2) * 512:(i % 2) * 512 + 512], wq[i],
                       sem="ql", wait=[("qc", i - 1)] +
                                      ([("kc", 64)] if i < 2 else []))
            elif i == 64:
                p.rdma("sync", WSB[:, 0:512], wq[0], sem="ql",
                       wait=[("qc", 63)])
        emit_qload(0)
        for fc in range(DC):
            for g in range(4):
                i = fc * 4 + g
                slot = WSB[:, (i % 2) * 512:(i % 2) * 512 + 512]
                emit_qload(i + 1)
                lv = p.v("ql")

                def mmq(E, g=g, slot=slot):
                    last = None
                    for j in range(4):
                        dc = g * 4 + j
                        for ti in range(NQT):
                            last = E.matmul(
                                PS[4 + ti][:, :], slot[:, j * 128:(j + 1) * 128],
                                A1[:, dc * SQ + ti * QT: dc * SQ + (ti + 1) * QT],
                                start=(dc == 0), stop=(dc == DC - 1))
                    return last
                p.op("tensor", mmq, sem="qc",
                     wait=[("ql", lv)] + ([("qev", p.v("qev"))] if g == 0 else []))
            qch = A2[:, fc * SQ:(fc + 1) * SQ]

            def evq(E, qch=qch):
                E.activation(out=qch[:, 0:512], in_=PS[4][:, :], func=ACTF.Copy)
                return E.activation(out=qch[:, 512:1024], in_=PS[5][:, :],
                                    func=ACTF.Copy)
            p.op("scalar", evq, sem="qev",
                 wait=[("qc", p.v("qc")), ("wvl", p.v("wvl")), ("vmm", VMMALL)])
            rv = rope_ops("q", qch, [("qev", p.v("qev")),
                                     ("qnm", p.v("qnm")), ("knm", p.v("knm"))])
            p.op("scalar", lambda E, qch=qch: E.activation(
                out=ksw[:, 0:1024], in_=qch.bitcast(F32), func=ACTF.Square),
                sem="qsq", wait=[("qr", rv), ("ksq", p.v("ksq")),
                                 ("knm", p.v("knm")), ("qnm", p.v("qnm"))])

            def qnmm(E):
                E.matmul(PS[6][0:1, :], OC, ksw[:, 0:512], start=True, stop=True)
                return E.matmul(PS[7][0:1, :], OC, ksw[:, 512:1024],
                                start=True, stop=True)
            p.op("tensor", qnmm, sem="qnm", wait=[("qsq", p.v("qsq")),
                                                  ("qnr", p.v("qnr"))])

            def qnred(E, fc=fc):
                E.tensor_reduce(out=R1[0:1, 16 + fc:17 + fc], in_=PS[6][0:1, :],
                                axis=AXX, op=ALU.max)
                E.tensor_reduce(out=R1[0:1, 2:3], in_=PS[7][0:1, :], axis=AXX,
                                op=ALU.max)
                return E.tensor_tensor(out=R1[0:1, 16 + fc:17 + fc],
                                       in0=R1[0:1, 16 + fc:17 + fc],
                                       in1=R1[0:1, 2:3], op=ALU.max)
            p.op("vector", qnred, sem="qnr", wait=[("qnm", p.v("qnm"))])
        QNR = p.v("qnr")
        if DEBUG_DUMPS and N_SUBLAYERS == 1:
            p.dma("sync", qcp.ap(), A2[:].bitcast(F32), sem="dbgq",
                  wait=[("qr", p.v("qr"))])

        p.dma("sync", RT[0:1, 0:128].rearrange("a (j t) -> a j t", t=16),
              kvag.ap().rearrange("(j r) t -> j r t", r=4097)[:, 4096, 0:16]
              .rearrange("j t -> () j t").bitcast(F32R),
              sem="kng", wait=[("cc", ag1)])

        def cbrow(E):
            E.tensor_reduce(out=RT[0:1, 128:144],
                            in_=RT[0:1, 0:128].rearrange("a (j h) -> a h j", h=16),
                            axis=AXX, op=ALU.max)
            return E.tensor_tensor(out=RT[0:1, 128:144], in0=RT[0:1, 128:144],
                                   in1=R1[0:1, 16:32], op=ALU.mult)
        p.op("vector", cbrow, sem="cbv", wait=[("kng", p.v("kng")), ("qnr", QNR)])
        p.op("scalar", lambda E: E.activation(
            out=RT[0:1, 128:144], in_=RT[0:1, 128:144], func=ACTF.Sqrt),
            sem="cbs", wait=[("cbv", p.v("cbv"))])
        p.op("vector", lambda E: E.tensor_scalar(
            out=RT[0:1, 128:144], in0=RT[0:1, 128:144], scalar1=-ISQ, scalar2=None,
            op0=ALU.mult), sem="cbn", wait=[("cbs", p.v("cbs"))])
        p.op("tensor", lambda E: mmrow(E, PS[6], RT, 128, 16), sem="cbm",
             wait=[("cbn", p.v("cbn")), ("qnm", p.v("qnm"))])
        p.op("vector", lambda E: E.tensor_copy(out=CBt[:, :], in_=PS[6][:, 0:16]),
             sem="cbb", wait=[("cbm", p.v("cbm"))])
        CBB = p.v("cbb")

        ksl = EX[:, 0:1024]
        vsl = EX[:, 1024:2048]
        kvagr = kvag.ap().rearrange("(j r) t -> j r t", r=4097)
        for h in range(NH):
            for j in range(NCORES):
                it = h * NCORES + j
                for hf in range(2):
                    p.rdma("sync", ksl[:, hf * 512:(hf + 1) * 512],
                           kvagr[j, h * 128:(h + 1) * 128, hf * 512:(hf + 1) * 512],
                           sem="kvl",
                           wait=[("sm", 8 * it), ("pv", 8 * it)] +
                                ([("qsq", p.v("qsq")), ("kr", p.v("kr")),
                                  ("qr", p.v("qr")), ("qnm", p.v("qnm")),
                                  ("knm", p.v("knm")), ("kdma", p.v("kdma")),
                                  ("ksq", p.v("ksq"))] if it == 0 else []))
                    vfl = kvag.ap().rearrange("r t -> (r t)")[
                        (j * 4097 + 2048) * SQ:(j * 4097 + 4096) * SQ].rearrange(
                        "(t f) -> t f", f=D)
                    p.rdma("sync",
                           vsl[:, hf * 512:(hf + 1) * 512].rearrange(
                               "p (i f) -> p i f", f=128),
                           vfl[hf * 512:(hf + 1) * 512, h * 128:(h + 1) * 128]
                           .rearrange("(i p) f -> p i f", p=128),
                           sem="kvl", wait=())
                kvlv = p.v("kvl")
                for c in range(8):
                    hf = c // 4
                    b0 = 2 * (c % 2)

                    def smm(E, h=h, c=c, hf=hf, b0=b0):
                        last = None
                        for ti in range(NQT):
                            last = E.matmul(
                                PS[b0 + ti][:, :],
                                ksl[:, hf * 512 + (c % 4) * 128:
                                    hf * 512 + (c % 4) * 128 + 128],
                                A2[:, h * SQ + ti * QT: h * SQ + (ti + 1) * QT],
                                start=True, stop=True)
                        return last
                    p.op("tensor", smm, sem="sm",
                         wait=[("kvl", kvlv),
                               ("pe", p.v("pe") - 2)])
                    smv = p.v("sm")
                    for ti in range(NQT):
                        p.op("scalar", lambda E, b0=b0, ti=ti, h=h: E.activation(
                            out=WSB[:, ti * 512:(ti + 1) * 512], in_=PS[b0 + ti][:, :],
                            func=ACTF.Exp, bias=CBt[:, h:h + 1], scale=ISQ),
                            sem="pe",
                            wait=[("sm", smv), ("pv", p.v("pv")), ("cbb", CBB)] +
                                 ([("qc", 64)] if it == 0 and c == 0 else []))
                    pev = p.v("pe")

                    def pvmm(E, h=h, j=j, c=c, hf=hf):
                        first = (j == 0 and c == 0)
                        last_ = (j == NCORES - 1 and c == 7)
                        vch = vsl[:, hf * 512 + (c % 4) * 128:
                                  hf * 512 + (c % 4) * 128 + 128]
                        for ti in range(NQT):
                            E.matmul(PS[4 + ti][:, :], vch,
                                     WSB[:, ti * 512:(ti + 1) * 512],
                                     start=first, stop=last_)
                        last = None
                        for ti in range(NQT):
                            last = E.matmul(PS[6 + ti][0:1, :], OC,
                                            WSB[:, ti * 512:(ti + 1) * 512],
                                            start=first, stop=last_)
                        return last
                    p.op("tensor", pvmm, sem="pv",
                         wait=[("pe", pev)] +
                              ([("hv", h)] if (j == 0 and c == 0) else []))
            pvv = p.v("pv")

            def sumev(E):
                srow = BCT[0:1, 1024:2048]
                E.tensor_copy(out=srow[0:1, 0:512], in_=PS[6][0:1, :])
                E.tensor_copy(out=srow[0:1, 512:1024], in_=PS[7][0:1, :])
                E.reciprocal(out=srow, in_=srow)
                return E.tensor_copy(out=RT[0:1, 0:1024], in_=srow)
            p.op("vector", sumev, sem="se", wait=[("pv", pvv)])

            def rbc(E):
                mmrow(E, PS[6], RT, 0, 512)
                return mmrow(E, PS[7], RT, 512, 512)
            p.op("tensor", rbc, sem="rb", wait=[("se", p.v("se"))])

            def ctxev(E, h=h):
                E.tensor_copy(out=BCT[:, 0:512], in_=PS[6][:, :])
                E.tensor_copy(out=BCT[:, 512:1024], in_=PS[7][:, :])
                E.tensor_tensor(out=A1[:, h * SQ: h * SQ + 512], in0=PS[4][:, :],
                                in1=BCT[:, 0:512], op=ALU.mult)
                return E.tensor_tensor(out=A1[:, h * SQ + 512: h * SQ + 1024],
                                       in0=PS[5][:, :], in1=BCT[:, 512:1024],
                                       op=ALU.mult)
            p.op("vector", ctxev, sem="hv", wait=[("rb", p.v("rb"))])
        HV = p.v("hv")

        lnv2 = ln_phase(A2, double=False, wait_extra=[("hv", HV)])
        for fc in range(DC):
            for g in range(4):
                i = fc * 4 + g
                slot = WSB[:, (i % 2) * 512:(i % 2) * 512 + 512]
                p.rdma("sync", slot, wo[i], sem="ol",
                       wait=[("oc", i - 1)] + ([("hv", HV)] if i < 2 else []) +
                            ([("dw0", p.v("dw0"))] if i == 2 and DEBUG_DUMPS and
                             N_SUBLAYERS == 1 else []))
                lv = p.v("ol")
                if DEBUG_DUMPS and N_SUBLAYERS == 1 and i == 0:
                    p.op("vector", lambda E: E.tensor_copy(
                        out=A2[:, 1024:1536], in_=WSB[:, 0:512].bitcast(F32)),
                        sem="dw0", wait=[("ol", lv)])

                def mmo(E, g=g, slot=slot, fc=fc):
                    last = None
                    for j in range(4):
                        dc = g * 4 + j
                        for ti in range(NQT):
                            last = E.matmul(
                                PS[(fc % 2) * 2 + ti][:, :],
                                slot[:, j * 128:(j + 1) * 128],
                                A1[:, dc * SQ + ti * QT: dc * SQ + (ti + 1) * QT],
                                start=(dc == 0), stop=(dc == DC - 1))
                    return last
                p.op("tensor", mmo, sem="oc",
                     wait=[("ol", lv), ("lnd", lnv2)] +
                          ([("orr", p.v("orr") - 1)] if g == 0 else []))

            if DEBUG_DUMPS and N_SUBLAYERS == 1 and fc == 0:
                def dcp(E):
                    E.tensor_copy(out=A2[:, 0:512], in_=PS[0][:, :])
                    return E.tensor_copy(out=A2[:, 512:1024], in_=PS[1][:, :])
                p.op("vector", dcp, sem="dcp", wait=[("oc", p.v("oc"))])

            def resid(E, fc=fc):
                exf = EX[:, 0:1024]
                exff = exf.bitcast(F32)
                xc = XR[:, fc * SQ:(fc + 1) * SQ]
                E.tensor_tensor(out=exf, in0=xc, in1=aB, op=ALU.mult)
                E.tensor_tensor(out=exf, in0=exff, in1=bB, op=ALU.subtract)
                E.tensor_tensor(out=exf, in0=exff, in1=xc.bitcast(F32), op=ALU.add)
                E.tensor_tensor(out=xc[:, 0:512], in0=exff[:, 0:512],
                                in1=PS[(fc % 2) * 2][:, :], op=ALU.add)
                return E.tensor_tensor(out=xc[:, 512:1024], in0=exff[:, 512:1024],
                                       in1=PS[(fc % 2) * 2 + 1][:, :], op=ALU.add)
            p.op("vector", resid, sem="orr", wait=[("oc", p.v("oc"))])
        ATT_DONE = p.v("orr")
        if DEBUG_DUMPS and N_SUBLAYERS == 1:
            p.dma("sync", kvcp.ap(), kvag.ap(), sem="dbg2x", wait=[("cc", ag1)])
            p.dma("sync", dbg1.ap()[:, 0:1536], A2[:, 0:1536].bitcast(F32),
                  sem="dbg", wait=[("orr", ATT_DONE), ("dcp", p.v("dcp")),
                                   ("dw0", p.v("dw0"))])
            p.dma("sync", dbg2.ap(), BCT[:, :], sem="dbg")
            p.dma("sync", dbg3.ap(), A1[:].bitcast(F32), sem="dbg")

    ffn_i = [0]

    def phase_ffn(prev_sem, prev_val):
        k = ffn_i[0]
        ffn_i[0] += 1
        t = f"f{k}"
        lnv = ln_phase(A2, double=False, wait_extra=[(prev_sem, prev_val)])
        for c in range(DC):
            yc = A1[:, c * SQ:(c + 1) * SQ]

            def app(E, c=c, yc=yc):
                E.tensor_tensor(out=yc, in0=XR[:, c * SQ:(c + 1) * SQ],
                                in1=aB, op=ALU.mult)
                E.tensor_tensor(out=yc, in0=yc.bitcast(F32), in1=bB, op=ALU.subtract)
                E.tensor_reduce(out=RT[:, 1025:1026], in_=yc.bitcast(F32),
                                axis=AXX, op=ALU.max, apply_absolute_value=True)
                if c == 0:
                    return E.tensor_copy(out=RT[:, 1024:1025],
                                         in_=RT[:, 1025:1026].bitcast(F32))
                return E.tensor_tensor(out=RT[:, 1024:1025],
                                       in0=RT[:, 1024:1025].bitcast(F32),
                                       in1=RT[:, 1025:1026].bitcast(F32), op=ALU.max)
            p.op("vector", app, sem=t + "ya", wait=[("lnd", lnv)] if c == 0 else ())
        YA = p.v(t + "ya")
        p.op("tensor", lambda E: E.transpose(
            PS[6][0:1, 0:128].bitcast(F32R),
            RT[:, 1024:1025], ident),
            sem=t + "tx", wait=[(t + "ya", YA)])

        def amrow(E):
            E.tensor_scalar(out=R1[0:1, 8:16], in0=R1[0:1, 48:56].bitcast(F32),
                            scalar1=0.0, scalar2=None, op0=ALU.mult)
            return E.tensor_reduce(out=R1[0:1, 8:9], in_=PS[6][0:1, 0:128], axis=AXX,
                                   op=ALU.max)
        p.op("vector", amrow, sem=t + "am", wait=[(t + "tx", p.v(t + "tx"))])
        p.dma("sync", amxb[k].ap(), R1[0:1, 8:16].bitcast(F32), sem=t + "ab",
              wait=[(t + "am", p.v(t + "am"))])
        p.dma("sync", R1[0:1, 24:32], amxb[k].ap().bitcast(F32R), sem=t + "ab",
              wait=())
        arv = p.collective("AllReduce", ALU.max, [amxb[k].ap()], [amxo[k].ap()],
                           sem="cc", wait=[(t + "ab", p.v(t + "ab"))])
        p.dma("sync", R1[0:1, 8:16], amxo[k].ap().bitcast(F32R), sem=t + "ag",
              wait=[("cc", arv)])
        p.dma("sync", R1[0:1, 24:32], amxo[k].ap().bitcast(F32R), sem=t + "ag",
              wait=())

        crow = BCT[0:1, 1024:1040]
        p.op("vector", lambda E, c=crow: E.tensor_tensor(
            out=c[0:1, 0:1], in0=R1[0:1, 8:9].bitcast(F32),
            in1=R1[0:1, 1:2].bitcast(F32), op=ALU.max), sem=t + "s1",
            wait=[(t + "ag", p.v(t + "ag")), ("faw", p.v("faw"))])

        def scl2(E, c=crow):
            E.tensor_tensor(out=c[0:1, 1:2], in0=R1[0:1, 8:9].bitcast(F32),
                            in1=c[0:1, 0:1], op=ALU.mult)
            E.tensor_copy(out=c[0:1, 2:3], in_=R1[0:1, 8:9].bitcast(F32))
            return E.reciprocal(out=c[0:1, 3:4], in_=c[0:1, 0:1])
        p.op("vector", scl2, sem=t + "s2", wait=[(t + "s1", p.v(t + "s1"))])

        def scl3(E, c=crow):
            E.tensor_scalar(out=c[0:1, 1:2], in0=c[0:1, 1:2],
                            scalar1=4.0 / (FP8M * FP8M), scalar2=None, op0=ALU.mult)
            E.reciprocal(out=c[0:1, 2:3], in_=c[0:1, 2:3])
            return E.tensor_copy(out=c[0:1, 4:5], in_=c[0:1, 3:4])
        p.op("vector", scl3, sem=t + "s3", wait=[(t + "s2", p.v(t + "s2"))])

        def scl4(E, c=crow):
            E.tensor_scalar(out=c[0:1, 2:3], in0=c[0:1, 2:3], scalar1=0.5 * FP8M,
                            scalar2=None, op0=ALU.mult)
            return E.tensor_scalar(out=c[0:1, 4:5], in0=c[0:1, 4:5],
                                   scalar1=0.5 * FP8M, scalar2=None, op0=ALU.mult)
        p.op("vector", scl4, sem=t + "s4", wait=[(t + "s3", p.v(t + "s3"))])

        def scl5(E, c=crow):
            E.tensor_copy(out=R1[0:1, 11:12], in_=c[0:1, 2:3])
            E.tensor_copy(out=R1[0:1, 12:13], in_=c[0:1, 4:5])
            return E.tensor_copy(out=R1[0:1, 10:11], in_=c[0:1, 1:2])
        p.op("vector", scl5, sem=t + "sc", wait=[(t + "s4", p.v(t + "s4"))])

        def sclb(E):
            mmrow(E, PS[6], R1, 11, 1)
            mmrow(E, PS[7], R1, 12, 1)
            return mmrow(E, PS[5], R1, 10, 1)
        p.op("tensor", sclb, sem=t + "sb", wait=[(t + "sc", p.v(t + "sc"))])

        def sclc(E):
            E.tensor_copy(out=SM[:, SM_SCL + 0:SM_SCL + 1], in_=PS[6][:, 0:1])
            E.tensor_copy(out=SM[:, SM_SCL + 1:SM_SCL + 2], in_=PS[7][:, 0:1])
            return E.tensor_copy(out=SM[:, SM_SCL + 2:SM_SCL + 3], in_=PS[5][:, 0:1])
        p.op("vector", sclc, sem=t + "sv", wait=[(t + "sb", p.v(t + "sb"))])
        SV = p.v(t + "sv")

        for c in range(DC):
            p.op("scalar", lambda E, c=c: E.activation(
                out=QS[:, 0:SQ], in_=A1[:, c * SQ:(c + 1) * SQ].bitcast(F32),
                func=ACTF.Identity, scale=SM[:, SM_SCL + 0:SM_SCL + 1]),
                sem=t + "xq", wait=[(t + "sv", SV), (t + "x8", p.v(t + "x8"))])
            p.op("vector", lambda E, c=c: E.tensor_copy(
                out=A1[:, c * SQ:(c + 1) * SQ], in_=QS[:, 0:SQ]),
                sem=t + "x8", wait=[(t + "xq", p.v(t + "xq"))])
        X8 = p.v(t + "x8")
        x8f = A1
        w8f = [EX[:, 1024:1536], EX[:, 1536:2048]]
        for oc in range(DC):
            bsel = (oc % 2) * 2
            for g in range(4):
                i = oc * 4 + g
                slot = WSB[:, (i % 2) * 512:(i % 2) * 512 + 512]
                p.rdma("sync", slot, ffnw[i], sem=t + "wl",
                       wait=[(t + "wq", i - 1)] +
                            ([(prev_sem, prev_val), (t + "sv", SV)] if i < 2 else []))
                p.op("scalar", lambda E, slot=slot: E.activation(
                    out=WS8[:, 0:512], in_=slot.bitcast(F32), func=ACTF.Identity,
                    scale=SM[:, SM_SCL + 1:SM_SCL + 2]),
                    sem=t + "wq",
                    wait=[(t + "wl", p.v(t + "wl")), (t + "w8", p.v(t + "w8"))])
                p.op("vector", lambda E, i=i: E.tensor_copy(
                    out=w8f[i % 2], in_=WS8[:, 0:512]),
                    sem=t + "w8",
                    wait=[(t + "wq", p.v(t + "wq")), (t + "mm", p.v(t + "mm") - 1)])

                def fmm(E, oc=oc, g=g, bsel=bsel, i=i):
                    last = None
                    for j in range(4):
                        dc = g * 4 + j
                        for ti in range(NQT):
                            last = E.matmul(
                                PS[bsel + ti][:, :],
                                w8f[i % 2][:, j * 128:(j + 1) * 128],
                                x8f[:, dc * SQ + ti * QT: dc * SQ + (ti + 1) * QT],
                                start=(dc == 0), stop=(dc == DC - 1))
                    return last
                p.op("tensor", fmm, sem=t + "mm",
                     wait=[(t + "w8", p.v(t + "w8")), (t + "x8", X8)] +
                          ([(t + "ep", p.v(t + "ep") - 1)] if (oc >= 2 and g == 0) else []))

            def fep(E, oc=oc, bsel=bsel):
                E.activation(out=EX[:, 0:512], in_=PS[bsel][:, :], func=ACTF.Identity,
                             bias=SM[:, SM_FFB + oc:SM_FFB + oc + 1],
                             scale=SM[:, SM_SCL + 2:SM_SCL + 3])
                return E.activation(out=EX[:, 512:1024], in_=PS[bsel + 1][:, :],
                                    func=ACTF.Identity,
                                    bias=SM[:, SM_FFB + oc:SM_FFB + oc + 1],
                                    scale=SM[:, SM_SCL + 2:SM_SCL + 3])
            p.op("scalar", fep, sem=t + "ev",
                 wait=[(t + "mm", p.v(t + "mm")), (t + "ep", p.v(t + "ep") - 1)])

            def fadd(E, oc=oc):
                xc = XR[:, oc * SQ:(oc + 1) * SQ]
                return E.tensor_tensor(out=xc, in0=xc.bitcast(F32),
                                       in1=EX[:, 0:1024].bitcast(F32), op=ALU.add)
            p.op("vector", fadd, sem=t + "ep", wait=[(t + "ev", p.v(t + "ev"))])
        return p.v(t + "ep")

    if N_SUBLAYERS >= 2:
        FFN1_DONE = phase_ffn("orr", ATT_DONE)
        if DEBUG_DUMPS and N_SUBLAYERS == 2:
            p.dma("sync", dbg1.ap()[0:1, 0:64], R1[0:1, :].bitcast(F32),
                  sem="dbg", wait=[("f0ep", FFN1_DONE)])
            p.dma("sync", dbg2.ap()[:, 0:328], SM[:, :], sem="dbg")
            p.dma("sync", dbg3.ap(), A1[:].bitcast(F32), sem="dbg")
            p.dma("sync", dbg2.ap()[:, 1000:1016], BCT[:, 1024:1040], sem="dbg")

    if N_SUBLAYERS >= 3:
        p.dma("scalar", xspill.ap(), XR[:].bitcast(F32), sem="msp",
              wait=[("f0ep", FFN1_DONE)])
        MSP = p.v("msp")
        lnv = ln_phase(A2, double=False, wait_extra=[("f0ep", FFN1_DONE)])
        for c in range(DC):
            def app(E, c=c):
                xc = XR[:, c * SQ:(c + 1) * SQ]
                E.tensor_tensor(out=xc, in0=xc.bitcast(F32), in1=aB, op=ALU.mult)
                return E.tensor_tensor(out=xc, in0=xc.bitcast(F32), in1=bB,
                                       op=ALU.subtract)
            p.op("vector", app, sem="mh",
                 wait=[("lnd", lnv), ("msp", MSP)] if c == 0 else ())
        MH_ = p.v("mh")
        p.dma("sync", halob.ap().rearrange("p (c k) -> p c k", k=3),
              XR[:].bitcast(F32).rearrange("p (c t) -> p c t", t=SQ)[:, :, 1021:1024],
              sem="mhb", wait=[("mh", MH_)])
        agh = p.collective("AllGather", ALU.bypass, [halob.ap()], [haloag.ap()],
                           sem="cc", wait=[("mhb", p.v("mhb"))])

        for i in range(2):
            slot = WSB[:, i * 512:(i + 1) * 512]
            p.rdma("sync", slot, wdt[i], sem="dtl",
                   wait=[("mh", MH_), ("f0wq", p.v("f0wq"))])
        DTL = p.v("dtl")

        def dtmm(E):
            last = None
            for dc in range(DC):
                st = WSB[:, (dc // 8) * 512 + (dc % 8) * 64:
                         (dc // 8) * 512 + (dc % 8) * 64 + 64]
                for ti in range(NQT):
                    last = E.matmul(PS[ti][0:64, :], st,
                                    XR[:, dc * SQ + ti * QT: dc * SQ + (ti + 1) * QT],
                                    start=(dc == 0), stop=(dc == DC - 1))
            return last
        p.op("tensor", dtmm, sem="dtm", wait=[("dtl", DTL), ("mh", MH_)])

        def dtsp_a(E):
            E.activation(out=RT[0:64, 0:512], in_=PS[0][0:64, :], func=ACTF.Identity,
                         bias=S1[:, 0:1].bitcast(F32))
            E.activation(out=RT[0:64, 512:1024], in_=PS[1][0:64, :],
                         func=ACTF.Identity, bias=S1[:, 0:1].bitcast(F32))
            E.activation(out=EX[0:64, 0:1024], in_=RT[0:64, 0:1024].bitcast(F32R),
                         func=ACTF.Abs)
            E.activation(out=EX[0:64, 0:1024], in_=EX[0:64, 0:1024], func=ACTF.Exp,
                         scale=-1.0)
            E.activation(out=EX[0:64, 0:1024], in_=EX[0:64, 0:1024], func=ACTF.Ln,
                         bias=OCl[0:64, 0:1].bitcast(F32))
            return E.activation(out=RT[0:64, 0:1024].bitcast(F32R),
                                in_=RT[0:64, 0:1024].bitcast(F32R), func=ACTF.Relu)
        p.op("scalar", dtsp_a, sem="dtsa", wait=[("dtm", p.v("dtm"))])
        p.op("vector", lambda E: E.tensor_tensor(
            out=RT[0:64, 0:1024], in0=RT[0:64, 0:1024],
            in1=EX[0:64, 0:1024].bitcast(F32), op=ALU.add), sem="dts",
            wait=[("dtsa", p.v("dtsa"))])
        p.op("scalar", lambda E: E.activation(out=S1[:, 3:4], in_=S1[:, 1:2],
                                              func=ACTF.Exp), sem="dta",
             wait=[("xl", XL)])
        p.op("vector", lambda E: E.tensor_scalar(
            out=S1[:, 3:4], in0=S1[:, 3:4].bitcast(F32), scalar1=-1.0, scalar2=None, op0=ALU.mult),
            sem="dtn", wait=[("dta", p.v("dta"))])
        for tc in range(NTC):
            p.op("tensor", lambda E, tc=tc: E.transpose(
                PS[2][:, 0:64].bitcast(F32R),
                RT[0:64, tc * 128:(tc + 1) * 128].bitcast(F32R), ident[0:64, 0:64]),
                sem="dtt", wait=[("dts", p.v("dts")), ("dtv", p.v("dtv"))])
            p.op("scalar", lambda E, tc=tc: E.activation(
                out=EX[:, 1536 + tc * 64:1536 + (tc + 1) * 64], in_=PS[2][:, 0:64],
                func=ACTF.Copy), sem="dtv", wait=[("dtt", p.v("dtt"))])
        p.op("vector", lambda E: E.tensor_scalar(
            out=RT[0:64, 0:1024], in0=RT[0:64, 0:1024], scalar1=S1[:, 3:4].bitcast(F32),
            scalar2=None, op0=ALU.mult), sem="dak",
            wait=[("dtv", p.v("dtv")), ("dtn", p.v("dtn"))])
        for tc in range(NTC):
            p.op("tensor", lambda E, tc=tc: E.transpose(
                PS[2][:, 0:64].bitcast(F32R),
                RT[0:64, tc * 128:(tc + 1) * 128].bitcast(F32R), ident[0:64, 0:64]),
                sem="dat", wait=[("dak", p.v("dak")), ("dav", p.v("dav"))])
            p.op("scalar", lambda E, tc=tc: E.activation(
                out=EX[:, 512 + tc * 64:512 + (tc + 1) * 64],
                in_=PS[2][:, 0:64], func=ACTF.Copy), sem="dav",
                wait=[("dat", p.v("dat"))])
        DAV = p.v("dav")
        for tc in range(NTC):
            p.op("tensor", lambda E, tc=tc: E.matmul(
                PS[3][:, 0:64], tri,
                EX[:, 512 + tc * 64:512 + (tc + 1) * 64],
                start=True, stop=True), sem="stm",
                wait=[("dav", DAV), ("stv", p.v("stv"))])
            p.op("scalar", lambda E, tc=tc: E.activation(
                out=RT[:, tc * 64:(tc + 1) * 64].bitcast(F32R), in_=PS[3][:, 0:64],
                func=ACTF.Copy), sem="stv", wait=[("stm", p.v("stm"))])
        STV = p.v("stv")
        p.op("vector", lambda E: E.tensor_scalar(
            out=RT[:, 512:1024].bitcast(F32R), in0=RT[:, 0:512], scalar1=-1.0,
            scalar2=None, op0=ALU.mult), sem="nst", wait=[("stv", STV)])
        NSTV = p.v("nst")
        ST_ = RT[:, 0:512]
        nST = RT[:, 512:1024]
        dtT = EX[:, 1536:2048]
        p.op("vector", lambda E: E.tensor_copy(out=BCT[:, 0:512],
                                               in_=EX[:, 512:1024].bitcast(F32)),
             sem="dtp", wait=[("dav", DAV), ("lnd", lnv)])
        p.op("vector", lambda E: E.tensor_scalar(
            out=BCT[:, 512:1024], in0=EX[:, 512:1024].bitcast(F32), scalar1=-1.0,
            scalar2=None, op0=ALU.mult), sem="dtp", wait=())
        DTP = p.v("dtp")
        dtAT = BCT[:, 0:512]
        ndtAT = BCT[:, 512:1024]
        p.op("tensor", lambda E: E.matmul(PS[3][0:1, :], OC,
                                          EX[:, 512:1024], start=True, stop=True),
             sem="slm", wait=[("dav", DAV), ("stm", p.v("stm")),
                              ("stv", p.v("stv"))])
        p.op("vector", lambda E: E.tensor_copy(out=WSB[0:1, 64:576],
                                               in_=PS[3][0:1, :]), sem="sld",
             wait=[("slm", p.v("slm")), ("dtm", p.v("dtm"))])
        SLD = p.v("sld")
        slrow = WSB[0:1, 64:576]
        exrow = EX[0:1, 0:512]

        p.op("vector", lambda E: E.tensor_scalar(
            out=exrow[0:1, 0:64], in0=slrow[0:1, 0:64].bitcast(F32),
            scalar1=0.0, scalar2=None, op0=ALU.mult), sem="rfs",
            wait=[("sld", SLD)])
        for tc in range(1, NTC):
            p.op("vector", lambda E, tc=tc: E.tensor_tensor(
                out=exrow[0:1, tc * 64:(tc + 1) * 64],
                in0=exrow[0:1, (tc - 1) * 64:tc * 64].bitcast(F32),
                in1=slrow[0:1, (tc - 1) * 64:tc * 64].bitcast(F32),
                op=ALU.add), sem="rfs", wait=[("rfs", p.v("rfs"))])
        p.op("vector", lambda E: E.tensor_tensor(
            out=WSB[0:1, 576:640], in0=exrow[0:1, 448:512].bitcast(F32),
            in1=slrow[0:1, 448:512].bitcast(F32), op=ALU.add),
            sem="rof", wait=[("rfs", p.v("rfs"))])
        ROF = p.v("rof")
        p.dma("scalar", stb.ap()[NS:NS + 1, 0:64], WSB[0:1, 576:640].bitcast(F32),
              sem="lgd", wait=[("rof", ROF)])

        def slbm(E):
            mmrow(E, PS[4], slrow, 0, 512)
            return mmrow(E, PS[5], exrow, 0, 512)
        p.op("tensor", slbm, sem="slb", wait=[("rof", ROF)])

        def slbc(E):
            E.tensor_copy(out=BCT[:, 1536:2048], in_=PS[4][:, :])
            return E.tensor_copy(out=BCT[:, 1024:1536], in_=PS[5][:, :])
        p.op("vector", slbc, sem="slc", wait=[("slb", p.v("slb"))])
        SLC = p.v("slc")
        SLB = BCT[:, 1536:2048]
        RoffB = BCT[:, 1024:1536]

        p.op("tensor", lambda E: E.transpose(
            PS[4][0:1, 0:64].bitcast(F32R), S1[:, 2:3],
            ident[0:64, 0:64]), sem="dbm", wait=[("slc", SLC)])
        p.op("scalar", lambda E: E.activation(
            out=WSB[0:1, 640:704], in_=PS[4][0:1, 0:64], func=ACTF.Copy),
            sem="dbv", wait=[("dbm", p.v("dbm"))])
        p.op("tensor", lambda E: mmrow(E, PS[4], WSB[0:1, 640:704].bitcast(F32), 0, 64),
             sem="dbb", wait=[("dbv", p.v("dbv"))])
        p.op("vector", lambda E: E.tensor_copy(out=DBv[:, :], in_=PS[4][:, 0:64]),
             sem="dbc", wait=[("dbb", p.v("dbb"))])
        DBC = p.v("dbc")

        for j_ in range(NCORES):
            p.dma("sync", EX[:, 1027 + j_ * 48:1027 + (j_ + 1) * 48],
                  haloag.ap()[j_ * 128:(j_ + 1) * 128, :].bitcast(F32R),
                  sem="hld", wait=[("cc", agh)] if j_ == 0 else ())

        def hmask(E):
            last = None
            for j in range(NCORES):
                src = EX[:, 1027 + j * 48:1027 + j * 48 + 48].bitcast(F32)
                if j == 0:
                    last = E.tensor_scalar(out=EX[:, 1411:1459], in0=src,
                                           scalar1=HSL[:, j:j + 1],
                                           scalar2=None, op0=ALU.mult)
                else:
                    last = E.scalar_tensor_tensor(
                        out=EX[:, 1411:1459], in0=src,
                        scalar=HSL[:, j:j + 1],
                        in1=EX[:, 1411:1459].bitcast(F32),
                        op0=ALU.mult, op1=ALU.add)
            return last
        p.op("vector", hmask, sem="hmk", wait=[("hld", p.v("hld"))])
        HMK = p.v("hmk")
        haloh = EX[:, 1411:1459]

        fc_list = [("b", g, wbp, g * 4) for g in range(NG)] + \
                  [("c", g, wcp, g * 4) for g in range(NG)] + \
                  [("x", fc, wxp, fc * 4) for fc in range(32)]
        convsrc = EX[:, 0:1027]

        def x_of(fc):
            return (A1 if fc < DC else A2)[:, (fc % DC) * SQ:(fc % DC + 1) * SQ]

        for fi, (kind, idx, wsrc, p0) in enumerate(fc_list):
            cch = {"b": 32 + idx, "c": 40 + idx, "x": idx}[kind]
            for g in range(4):
                i = fi * 4 + g
                slot = WSB[:, (i % 2) * 512:(i % 2) * 512 + 512]
                p.rdma("sync", slot, wsrc[p0 + g], sem="pjl",
                       wait=[("pjc", i - 1)] +
                            ([("rof", ROF), ("dbc", DBC)] if i < 2 else []))
                lv = p.v("pjl")

                def mmp(E, g=g, slot=slot, fi=fi):
                    last = None
                    for j in range(4):
                        dc = g * 4 + j
                        for ti in range(NQT):
                            E.matmul(PS[(fi % 2) * 2 + ti][:, :],
                                     slot[:, j * 128:(j + 1) * 128],
                                     XR[:, dc * SQ + ti * QT: dc * SQ + (ti + 1) * QT],
                                     start=(dc == 0), stop=(dc == DC - 1))
                        last = E.matmul(PS[4 + (fi % 2)][:, 0:3],
                                        slot[:, j * 128:(j + 1) * 128].bitcast(F32),
                                        haloh[:, dc * 3:dc * 3 + 3].bitcast(F32),
                                        start=(dc == 0), stop=(dc == DC - 1))
                    return last
                p.op("tensor", mmp, sem="pjc",
                     wait=[("pjl", lv), ("hmk", HMK), ("mh", MH_)] +
                          ([("pcv", p.v("pcv") - 1)] if g == 0 else []))

            def pev(E, fi=fi):
                E.activation(out=convsrc[:, 0:3], in_=PS[4 + (fi % 2)][:, 0:3],
                             func=ACTF.Copy)
                E.activation(out=convsrc[:, 3:515], in_=PS[(fi % 2) * 2][:, :],
                             func=ACTF.Copy)
                return E.activation(out=convsrc[:, 515:1027],
                                    in_=PS[(fi % 2) * 2 + 1][:, :], func=ACTF.Copy)
            p.op("scalar", pev, sem="pje",
                 wait=[("pjc", p.v("pjc")), ("pcv", p.v("pcv"))])
            PJE = p.v("pje")
            if kind == "x":
                accf = x_of(idx)
                dst = x_of(idx)
            else:
                accf = A1[:, 0:1024]
                dst = A1[:, 0:1024]
            cs = convsrc.bitcast(F32)

            def cnv(E, accf=accf, cch=cch, cs=cs):
                E.tensor_scalar(out=accf, in0=cs[:, 0:1024],
                                scalar1=SM[:, SM_CVW + cch * 4:SM_CVW + cch * 4 + 1],
                                scalar2=SM[:, SM_CVB + cch:SM_CVB + cch + 1],
                                op0=ALU.mult, op1=ALU.add)
                last = None
                for kk in range(1, 4):
                    last = E.scalar_tensor_tensor(
                        out=accf, in0=cs[:, kk:kk + 1024],
                        scalar=SM[:, SM_CVW + cch * 4 + kk:SM_CVW + cch * 4 + kk + 1],
                        in1=accf.bitcast(F32), op0=ALU.mult, op1=ALU.add)
                return last
            p.op("vector", cnv, sem="pcv",
                 wait=[("pje", PJE), ("bcd", p.v("bcd")), ("btv", p.v("btv"))])
            p.op("scalar", lambda E, accf=accf, dst=dst: E.activation(
                out=dst, in_=accf.bitcast(F32), func=ACTF.Silu), sem="psl",
                wait=[("pcv", p.v("pcv"))])
            PSL = p.v("psl")
            if kind in ("b", "c"):
                sp = bspill if kind == "b" else cspill
                p.dma("scalar", sp.ap()[idx], A1[:, 0:1024].bitcast(F32), sem="bcd",
                      wait=[("psl", PSL)])
                if kind == "b":
                    for tc in range(NTC):
                        p.op("tensor", lambda E, tc=tc: E.transpose(
                            PS[6][:, 0:128].bitcast(F32R),
                            A1[:, tc * 128:(tc + 1) * 128], ident),
                            sem="btm", wait=[("psl", PSL), ("btv", p.v("btv"))])
                        p.op("scalar", lambda E: E.activation(
                            out=A1[:, 1024:1152], in_=PS[6][:, 0:128],
                            func=ACTF.Copy), sem="btv",
                            wait=[("btm", p.v("btm")), ("bcd", p.v("bcd"))])
                        p.dma("scalar", btspill.ap()[idx, tc],
                              A1[:, 1024:1152].bitcast(F32), sem="bcd",
                              wait=[("btv", p.v("btv"))])
        PJALL = p.v("psl")
        BCD = p.v("bcd")

        if DEBUG_DUMPS:
            p.dma("sync", dbg1.ap()[:, 0:1024], RT[:, 0:1024], sem="dbg",
                  wait=[("nst", NSTV), ("bcd", p.v("bcd")), ("slc", SLC)])
            p.dma("sync", dbg2.ap(), BCT[:, :], sem="dbg")
            p.dma("sync", dbg3.ap()[:, 1536:2048], EX[:, 1536:2048].bitcast(F32),
                  sem="dbg")
        states = EX[:, 0:1024]
        for ps_ in range(4):
            hs0 = ps_ * 16
            p.op("vector", lambda E: E.tensor_scalar(
                out=states[:], in0=A1[:, 0:1024].bitcast(F32), scalar1=0.0,
                scalar2=None, op0=ALU.mult), sem="sms",
                 wait=[("bcd", BCD), ("pcv", p.v("pcv")), ("std", p.v("std")),
                       ("slv", p.v("slv"))])
            SMS = p.v("sms")
            for tc in range(NTC):
                for hh in range(16):
                    h = hs0 + hh
                    g = h // 8
                    fc = h // 2
                    hl = (h % 2) * 64
                    sidx = hh * 64
                    xch = x_of(fc)
                    col = tc * 64 + h
                    if h % 8 == 0:
                        p.rdma("sync", RT[:, 1024:1152],
                               bspill.ap()[g, :, tc * 128:(tc + 1) * 128], sem="sbl",
                               wait=[("gcn", p.v("gcn"))])
                        p.rdma("sync", WSB[:, 128:256], btspill.ap()[g, tc],
                               sem="sbl", wait=[("bpp", p.v("bpp"))])
                        p.rdma("sync", RT[:, 1152:1280],
                               cspill.ap()[g, :, tc * 128:(tc + 1) * 128], sem="sbl",
                               wait=[("ccl", p.v("ccl"))])
                        sblv = p.v("sbl")
                        p.op("tensor", lambda E: E.matmul(
                            PS[0][:, 0:128], RT[:, 1024:1152].bitcast(F32R),
                            RT[:, 1152:1280].bitcast(F32R), start=True, stop=True),
                            sem="gmm", wait=[("sbl", sblv), ("sms", SMS),
                                             ("gcn", p.v("gcn"))])
                        p.op("vector", lambda E: E.tensor_tensor(
                            out=WSB[:, 0:128], in0=PS[0][:, 0:128],
                            in1=tri.bitcast(F32), op=ALU.mult), sem="gcn",
                            wait=[("gmm", p.v("gmm")), ("wcl", p.v("wcl"))])
                    GCN = p.v("gcn")
                    if h % 2 == 0:
                        p.op("tensor", lambda E, xch=xch, tc=tc: E.transpose(
                            PS[4][:, 0:128].bitcast(F32R),
                            xch[:, tc * 128:(tc + 1) * 128], ident),
                            sem="xtm", wait=[("psl", PJALL), ("xtv", p.v("xtv")),
                                             ("sms", SMS), ("dmm", p.v("dmm")),
                                             ("ymm", p.v("ymm"))])
                        for hj in range(2):
                            p.op("scalar", lambda E, hj=hj, col=col: E.activation(
                                out=WSB[:, 256 + hj * 64:256 + (hj + 1) * 64],
                                in_=PS[4][:, hj * 64:(hj + 1) * 64], func=ACTF.Identity,
                                scale=dtT[:, col + hj:col + hj + 1].bitcast(F32)),
                                sem="xtv", wait=[("xtm", p.v("xtm")),
                                                 ("dtv", p.v("dtv"))])
                    XTV = p.v("xtv")
                    p.op("vector", lambda E, col=col: E.tensor_scalar(
                        out=WSB[:, 768:896], in0=onesm.bitcast(F32),
                        scalar1=dtAT[:, col:col + 1], scalar2=None, op0=ALU.mult),
                        sem="w1b", wait=[("dtp", DTP), ("ccl", p.v("ccl"))])
                    p.op("vector", lambda E, col=col: E.tensor_scalar(
                        out=WSB[:, 640:768], in0=tri.bitcast(F32),
                        scalar1=ndtAT[:, col:col + 1], scalar2=None, op0=ALU.mult),
                        sem="w2b", wait=[("dtp", DTP), ("ccl", p.v("ccl"))])
                    p.op("tensor", lambda E: (
                        E.matmul(PS[1][:, 0:128], WSB[:, 768:896], tri,
                                 start=True, stop=False),
                        E.matmul(PS[1][:, 0:128], WSB[:, 640:768], onesm,
                                 start=False, stop=True),
                        E.matmul(PS[5][:, 0:128], WSB[:, 768:896], tri,
                                 start=True, stop=True))[-1], sem="sbm",
                        wait=[("w1b", p.v("w1b")), ("w2b", p.v("w2b")),
                              ("ecl", p.v("ecl"))])
                    p.op("vector", lambda E: E.tensor_tensor(
                        out=WSB[:, 384:512], in0=PS[1][:, 0:128],
                        in1=tri.bitcast(F32), op=ALU.mult), sem="eem",
                        wait=[("sbm", p.v("sbm")), ("dmm", p.v("dmm"))])
                    p.op("scalar", lambda E: E.activation(
                        out=WSB[:, 384:512], in_=WSB[:, 384:512].bitcast(F32R),
                        func=ACTF.Exp), sem="eex",
                        wait=[("eem", p.v("eem"))])
                    p.op("scalar", lambda E: E.activation(
                        out=WSB[:, 640:768], in_=PS[5][:, 0:128], func=ACTF.Exp),
                        sem="ecl", wait=[("eex", p.v("eex")), ("w2b", p.v("w2b")),
                                         ("ccl", p.v("ccl"))])
                    p.op("vector", lambda E: E.tensor_tensor(
                        out=WSB[:, 512:640], in0=WSB[:, 384:512].bitcast(F32),
                        in1=WSB[:, 0:128].bitcast(F32), op=ALU.mult),
                        sem="wcl", wait=[("eex", p.v("eex")), ("gcn", GCN),
                                         ("ymm", p.v("ymm"))])
                    p.op("vector", lambda E: E.tensor_tensor(
                        out=WSB[:, 768:896], in0=RT[:, 1152:1280],
                        in1=WSB[:, 640:768].bitcast(F32), op=ALU.mult), sem="ccl",
                        wait=[("ecl", p.v("ecl")), ("ymm", p.v("ymm"))])
                    spair = (sidx // 128) * 128
                    p.op("tensor", lambda E, spair=spair: (
                        E.matmul(PS[2][:, 0:128], WSB[:, 256:384],
                                 WSB[:, 512:640], start=True, stop=False),
                        E.matmul(PS[2][:, 0:128], states[:, spair:spair + 128],
                                 WSB[:, 768:896], start=False, stop=True))[-1],
                        sem="ymm", wait=[("wcl", p.v("wcl")), ("ccl", p.v("ccl")),
                                         ("xtv", XTV), ("yev", p.v("yev")),
                                         ("dsu", p.v("dsu"))])
                    p.op("vector", lambda E, xch=xch, tc=tc, hl=hl, h=h:
                         E.scalar_tensor_tensor(
                             out=xch[hl:hl + 64, tc * 128:(tc + 1) * 128],
                             in0=xch[hl:hl + 64, tc * 128:(tc + 1) * 128].bitcast(F32),
                             scalar=DBv[hl:hl + 64, h:h + 1],
                             in1=PS[2][hl:hl + 64, 0:128],
                             op0=ALU.mult, op1=ALU.add), sem="yev",
                         wait=[("ymm", p.v("ymm")), ("dbc", DBC)])
                    p.op("scalar", lambda E, col=col: (
                        E.activation(out=WSB[:, 896:897],
                                     in_=nST[:, col:col + 1].bitcast(F32R),
                                     func=ACTF.Exp, bias=SLB[:, col:col + 1]),
                        E.activation(out=WSB[:, 897:898],
                                     in_=SLB[:, col:col + 1].bitcast(F32R),
                                     func=ACTF.Exp))[-1], sem="exc",
                        wait=[("slc", SLC), ("dsu", p.v("dsu")),
                              ("bpp", p.v("bpp")), ("nst", NSTV)])
                    p.op("vector", lambda E: E.tensor_scalar(
                        out=WSB[:, 384:512], in0=WSB[:, 128:256].bitcast(F32),
                        scalar1=WSB[:, 896:897].bitcast(F32), scalar2=None,
                        op0=ALU.mult), sem="bpp",
                        wait=[("exc", p.v("exc")), ("wcl", p.v("wcl"))])
                    p.op("tensor", lambda E, h=h: E.matmul(
                        PS[3][:, 0:64], WSB[:, 384:512],
                        WSB[:, 256 + (h % 2) * 64:256 + (h % 2) * 64 + 64],
                        start=True, stop=True), sem="dmm",
                        wait=[("bpp", p.v("bpp")), ("dsu", p.v("dsu"))])
                    p.op("vector", lambda E, sidx=sidx: E.scalar_tensor_tensor(
                        out=states[:, sidx:sidx + 64],
                        in0=states[:, sidx:sidx + 64].bitcast(F32),
                        scalar=WSB[:, 897:898].bitcast(F32), in1=PS[3][:, 0:64],
                        op0=ALU.mult, op1=ALU.add), sem="dsu",
                        wait=[("dmm", p.v("dmm")), ("exc", p.v("exc"))])
            p.dma("sync", stb.ap()[0:NS, ps_ * 1024:(ps_ + 1) * 1024],
                  states[:].bitcast(F32), sem="std", wait=[("dsu", p.v("dsu"))])
        STD = p.v("std")

        agst = p.collective("AllGather", ALU.bypass, [stb.ap()], [stag.ap()],
                            sem="cc", wait=[("std", STD), ("lgd", p.v("lgd"))])

        p.dma("scalar", WSB[0:1, 0:512].rearrange("a (j c) -> a j c", c=64),
              stag.ap().rearrange("(j r) c -> j r c", r=NS + 1)[:, NS, 0:64]
              .rearrange("j c -> () j c").bitcast(F32R), sem="lgl",
              wait=[("cc", agst), ("dsu", p.v("dsu")), ("bpp", p.v("bpp"))])
        wrow = WSB[0:1, 0:512]
        prow = WSB[0:1, 512:1024]

        p.op("vector", lambda E: E.tensor_copy(out=prow[0:1, 0:64],
                                               in_=wrow[0:1, 0:64]),
             sem="wms", wait=[("lgl", p.v("lgl"))])
        for j in range(1, 8):
            p.op("vector", lambda E, j=j: E.tensor_tensor(
                out=prow[0:1, j * 64:(j + 1) * 64],
                in0=prow[0:1, (j - 1) * 64:j * 64],
                in1=wrow[0:1, j * 64:(j + 1) * 64], op=ALU.add),
                sem="wms", wait=[("wms", p.v("wms"))])
        p.op("vector", lambda E: E.tensor_tensor(
            out=wrow.rearrange("a (j h) -> a j h", h=64),
            in0=wrow.rearrange("a (j h) -> a j h", h=64),
            in1=R1[0:1, 48:56].rearrange("a j -> a j ()").broadcast_to([1, 8, 64]),
            op=ALU.mult), sem="wms", wait=[("wms", p.v("wms"))])
        p.op("vector", lambda E: E.tensor_reduce(
            out=BCT[0:1, 1536:1600],
            in_=wrow.rearrange("a (j h) -> a h j", h=64), axis=AXX,
            op=ALU.add), sem="wms", wait=[("wms", p.v("wms"))])
        p.op("vector", lambda E: E.tensor_tensor(
            out=prow.rearrange("a (j h) -> a j h", h=64),
            in0=BCT[0:1, 1536:1600].rearrange("a h -> a () h").broadcast_to([1, 8, 64]),
            in1=prow.rearrange("a (j h) -> a j h", h=64),
            op=ALU.subtract), sem="wms", wait=[("wms", p.v("wms"))])
        p.op("vector", lambda E: E.tensor_tensor(
            out=prow.rearrange("a (j h) -> a j h", h=64),
            in0=prow.rearrange("a (j h) -> a j h", h=64),
            in1=R1[0:1, 48:56].rearrange("a j -> a j ()").broadcast_to([1, 8, 64]),
            op=ALU.mult), sem="wmv", wait=[("wms", p.v("wms"))])
        p.op("scalar", lambda E: E.activation(out=prow, in_=prow, func=ACTF.Exp),
             sem="wme", wait=[("wmv", p.v("wmv"))])
        p.op("vector", lambda E: E.tensor_tensor(
            out=prow.rearrange("a (j h) -> a j h", h=64),
            in0=prow.rearrange("a (j h) -> a j h", h=64),
            in1=R1[0:1, 48:56].rearrange("a j -> a j ()").broadcast_to([1, 8, 64]),
            op=ALU.mult),
            sem="wmm", wait=[("wme", p.v("wme"))])
        p.op("tensor", lambda E: mmrow(E, PS[4], prow, 0, 512), sem="wbb",
             wait=[("wmm", p.v("wmm")), ("slb", p.v("slb"))])
        p.op("vector", lambda E: E.tensor_copy(out=BCT[:, 1536:2048],
                                               in_=PS[4][:, :]), sem="wbc",
             wait=[("wbb", p.v("wbb"))])
        WBC = p.v("wbc")
        WB = BCT[:, 1536:2048]

        stagr = stag.ap().rearrange("(j r) c -> j r c", r=NS + 1)
        for g in range(NG):
            for hh in range(8):
                h = g * 8 + hh
                for j_ in range(NCORES):
                    p.dma("sync",
                          EX[:, 1024 + j_ * 64:1024 + (j_ + 1) * 64],
                          stagr[j_, 0:NS, h * 64:(h + 1) * 64].bitcast(F32R),
                          sem="cgl", wait=([("cgm", p.v("cgm"))] if j_ == 0 else [])
                          + [("cc", agst)])

                for j in range(NCORES):
                    sl = EX[:, 1024 + j * 64:1024 + (j + 1) * 64].bitcast(F32)
                    if j == 0:
                        p.op("vector", lambda E, sl=sl, hh=hh, h=h: E.tensor_scalar(
                            out=WSB[:, hh * 64:(hh + 1) * 64], in0=sl,
                            scalar1=WB[:, h:h + 1],
                            scalar2=None, op0=ALU.mult), sem="cgs",
                            wait=[("cgl", p.v("cgl")), ("wbc", WBC)])
                    else:
                        p.op("vector", lambda E, sl=sl, hh=hh, h=h, j=j:
                             E.scalar_tensor_tensor(
                                 out=WSB[:, hh * 64:(hh + 1) * 64], in0=sl,
                                 scalar=WB[:, j * 64 + h:j * 64 + h + 1],
                                 in1=WSB[:, hh * 64:(hh + 1) * 64].bitcast(F32),
                                 op0=ALU.mult, op1=ALU.add),
                             sem="cgs" if j < NCORES - 1 else "cgm",
                             wait=[("cgs", p.v("cgs"))])
            CGM = p.v("cgm")
            for tc in range(NTC):
                p.rdma("sync", RT[:, 1152:1280],
                       cspill.ap()[g, :, tc * 128:(tc + 1) * 128], sem="crl",
                       wait=[("crc", p.v("crc"))])
                CRL = p.v("crl")
                for hh in range(8):
                    h = g * 8 + hh
                    fc = h // 2
                    hl = (h % 2) * 64
                    xch = x_of(fc)
                    col = tc * 64 + h
                    p.op("vector", lambda E, col=col: E.tensor_scalar(
                        out=WSB[:, 512:640], in0=onesm.bitcast(F32),
                        scalar1=dtAT[:, col:col + 1], scalar2=None, op0=ALU.mult),
                        sem="cw1", wait=[("csb", p.v("csb"))])
                    p.op("tensor", lambda E: E.matmul(
                        PS[1][:, 0:128], WSB[:, 512:640], tri,
                        start=True, stop=True), sem="csb",
                        wait=[("cw1", p.v("cw1")), ("cse", p.v("cse"))])
                    p.op("scalar", lambda E, col=col: E.activation(
                        out=WSB[:, 640:768], in_=PS[1][:, 0:128], func=ACTF.Exp,
                        bias=RoffB[:, col:col + 1]), sem="cse",
                        wait=[("csb", p.v("csb")), ("crc", p.v("crc"))])
                    p.op("vector", lambda E: E.tensor_tensor(
                        out=WSB[:, 768:896], in0=RT[:, 1152:1280],
                        in1=WSB[:, 640:768].bitcast(F32), op=ALU.mult), sem="crc",
                        wait=[("cse", p.v("cse")), ("crl", CRL), ("cgm", CGM),
                              ("cmm", p.v("cmm"))])
                    p.op("tensor", lambda E, hh=hh: E.matmul(
                        PS[2][:, 0:128], WSB[:, (hh // 2) * 128:(hh // 2) * 128 + 128],
                        WSB[:, 768:896], start=True, stop=True), sem="cmm",
                        wait=[("crc", p.v("crc")), ("cad", p.v("cad"))])
                    p.op("vector", lambda E, xch=xch, tc=tc, hl=hl: E.tensor_tensor(
                        out=xch[hl:hl + 64, tc * 128:(tc + 1) * 128],
                        in0=xch[hl:hl + 64, tc * 128:(tc + 1) * 128].bitcast(F32),
                        in1=PS[2][hl:hl + 64, 0:128], op=ALU.add), sem="cad",
                        wait=[("cmm", p.v("cmm"))])
        CORR = p.v("cad")

        for fc in range(32):
            for g in range(4):
                i = fc * 4 + g
                slot = WSB[:, (i % 2) * 512:(i % 2) * 512 + 512]
                p.rdma("sync", slot, wz[i], sem="gzl",
                       wait=[("gzc", i - 1)] + ([("cad", CORR)] if i < 2 else []))
                lv = p.v("gzl")

                def mmz(E, g=g, slot=slot, fc=fc):
                    last = None
                    for j in range(4):
                        dc = g * 4 + j
                        for ti in range(NQT):
                            last = E.matmul(
                                PS[(fc % 2) * 2 + ti][:, :],
                                slot[:, j * 128:(j + 1) * 128],
                                XR[:, dc * SQ + ti * QT: dc * SQ + (ti + 1) * QT],
                                start=(dc == 0), stop=(dc == DC - 1))
                    return last
                p.op("tensor", mmz, sem="gzc",
                     wait=[("gzl", lv)] + ([("gsl", p.v("gsl") - 1)] if g == 0 else []))

            def gsl(E, fc=fc):
                E.activation(out=RT[:, 0:512].bitcast(F32R), in_=PS[(fc % 2) * 2][:, :],
                             func=ACTF.Silu)
                return E.activation(out=RT[:, 512:1024].bitcast(F32R),
                                    in_=PS[(fc % 2) * 2 + 1][:, :], func=ACTF.Silu)
            p.op("scalar", gsl, sem="gsl",
                 wait=[("gzc", p.v("gzc")), ("gmu", p.v("gmu"))])
            ych = x_of(fc)
            p.op("vector", lambda E, ych=ych: E.tensor_tensor(
                out=ych, in0=ych.bitcast(F32), in1=RT[:, 0:1024], op=ALU.mult),
                sem="gmu", wait=[("gsl", p.v("gsl")), ("cad", CORR)])
            p.op("scalar", lambda E, ych=ych: E.activation(
                out=EX[:, 0:1024], in_=ych, func=ACTF.Square), sem="gsq",
                wait=[("gmu", p.v("gmu"))])

            def gss(E, fc=fc):
                E.matmul(PS[6][0:1, :], OC, EX[:, 0:512], start=(fc == 0),
                         stop=(fc == 31))
                return E.matmul(PS[7][0:1, :], OC, EX[:, 512:1024], start=(fc == 0),
                                stop=(fc == 31))
            p.op("tensor", gss, sem="gss", wait=[("gsq", p.v("gsq"))])
        GMU = p.v("gmu")

        def rsr(E):
            srow = BCT[0:1, 1024:2048]
            E.tensor_copy(out=srow[0:1, 0:512], in_=PS[6][0:1, :])
            return E.tensor_copy(out=srow[0:1, 512:1024], in_=PS[7][0:1, :])
        p.op("vector", rsr, sem="rs1", wait=[("gss", p.v("gss"))])
        p.op("scalar", lambda E: E.activation(
            out=BCT[0:1, 1024:2048], in_=BCT[0:1, 1024:2048], func=ACTF.Sqrt,
            bias=EPC[0:1, 0:1], scale=1.0 / MI),
            sem="rs2a", wait=[("rs1", p.v("rs1"))])

        def rs2f(E):
            E.reciprocal(out=BCT[0:1, 1024:2048], in_=BCT[0:1, 1024:2048])
            return E.tensor_copy(out=RT[0:1, 0:1024], in_=BCT[0:1, 1024:2048])
        p.op("vector", rs2f, sem="rs2", wait=[("rs2a", p.v("rs2a"))])

        def rsb(E):
            mmrow(E, PS[4], RT, 0, 512)
            return mmrow(E, PS[5], RT, 512, 512)
        p.op("tensor", rsb, sem="rs3", wait=[("rs2", p.v("rs2"))])

        def rsc(E):
            E.tensor_copy(out=BCT[:, 0:512], in_=PS[4][:, :])
            return E.tensor_copy(out=BCT[:, 512:1024], in_=PS[5][:, :])
        p.op("vector", rsc, sem="rs4", wait=[("rs3", p.v("rs3"))])
        RS4 = p.v("rs4")
        rstdB = BCT[:, 0:1024]
        p.dma("scalar", XR[:], xspill.ap().bitcast(F32R), sem="xrl",
              wait=[("gmu", GMU), ("dtm", p.v("dtm"))])
        XRL = p.v("xrl")

        for dc in range(32):
            ych = x_of(dc)
            p.op("vector", lambda E, ych=ych, dc=dc: E.scalar_tensor_tensor(
                out=ych, in0=ych.bitcast(F32),
                scalar=SM[:, SM_NRMW + dc:SM_NRMW + dc + 1], in1=rstdB,
                op0=ALU.mult, op1=ALU.mult), sem="gnm",
                wait=[("rs4", RS4)] if dc == 0 else ())
        GNM = p.v("gnm")
        for fc in range(DC):
            for g in range(8):
                i = fc * 8 + g
                slot = WSB[:, (i % 2) * 512:(i % 2) * 512 + 512]
                p.rdma("sync", slot, wom[i], sem="wol",
                       wait=[("woc", i - 1)] + ([("gzc", 128)] if i < 2 else []))
                lv = p.v("wol")

                def mmw(E, g=g, slot=slot, fc=fc):
                    last = None
                    for j in range(4):
                        dc = g * 4 + j
                        ych = x_of(dc)
                        for ti in range(NQT):
                            last = E.matmul(
                                PS[(fc % 2) * 2 + ti][:, :],
                                slot[:, j * 128:(j + 1) * 128],
                                ych[:, ti * QT:(ti + 1) * QT],
                                start=(dc == 0), stop=(dc == 31))
                    return last
                p.op("tensor", mmw, sem="woc",
                     wait=[("wol", lv), ("gnm", GNM)] +
                          ([("m9r", p.v("m9r") - 1)] if g == 0 else []))

            def madd(E, fc=fc):
                xc = XR[:, fc * SQ:(fc + 1) * SQ]
                E.tensor_tensor(out=xc[:, 0:512], in0=xc[:, 0:512].bitcast(F32),
                                in1=PS[(fc % 2) * 2][:, :], op=ALU.add)
                return E.tensor_tensor(out=xc[:, 512:1024],
                                       in0=xc[:, 512:1024].bitcast(F32),
                                       in1=PS[(fc % 2) * 2 + 1][:, :], op=ALU.add)
            p.op("vector", madd, sem="m9r", wait=[("woc", p.v("woc")), ("xrl", XRL)])
        MAMBA_DONE = p.v("m9r")

    if N_SUBLAYERS >= 4:
        FFN2_DONE = phase_ffn("m9r", MAMBA_DONE)
        final = ("f1ep", FFN2_DONE)
    elif N_SUBLAYERS == 3:
        final = ("m9r", MAMBA_DONE)
    elif N_SUBLAYERS == 2:
        final = ("f0ep", FFN1_DONE)
    else:
        final = ("orr", ATT_DONE)

    for c in range(DC):
        p.dma("sync", yout[c * 128:(c + 1) * 128, :],
              XR[:, c * SQ:(c + 1) * SQ].bitcast(F32), sem="yo",
              wait=[final] if c == 0 else ())
    p.wait("gpsimd", [("yo", p.v("yo"))] +
           ([("dbg", p.v("dbg"))] if p.v("dbg") else []))
    if p.v("dbg2x"):
        p.wait("sync", [("dbg2x", p.v("dbg2x")), ("dbgq", p.v("dbgq"))])

    with nc.Block() as block:
        @block.sync
        def _(E):
            for fn in p.q["sync"]:
                fn(E)

        @block.gpsimd
        def _(E):
            for fn in p.q["gpsimd"]:
                fn(E)

        @block.scalar
        def _(E):
            for fn in p.q["scalar"]:
                fn(E)

        @block.vector
        def _(E):
            for fn in p.q["vector"]:
                fn(E)

        @block.tensor
        def _(E):
            for fn in p.q["tensor"]:
                fn(E)
    return nc


def _pieces_st(w, n_in, n_out):
    nic, noc = n_in // 128, n_out // 128
    a = np.asarray(w, dtype=np.float32).reshape(nic, 128, noc, 128).transpose(2, 0, 1, 3)
    a = a.reshape(noc, nic // 4, 4, 128, 128).transpose(0, 1, 3, 2, 4)
    return np.ascontiguousarray(a.reshape(noc * (nic // 4), 128, 512))


def _rope_perm():
    per = np.concatenate([np.arange(0, 64, 2), np.arange(1, 64, 2),
                          np.arange(64, 128)])
    return np.concatenate([h * 128 + per for h in range(NH)])


def kernel(**inputs):
    X = np.asarray(inputs["L0_attn_a_X"])[0]
    W_in = np.asarray(inputs["L0_attn_a_W_in"])
    perm = _rope_perm()
    wq_p = _pieces_st(W_in[:, 0:D][:, perm], D, D)
    wk_p = _pieces_st(W_in[:, D:2 * D][:, perm], D, D)
    Wv = np.asarray(W_in[:, 2 * D:3 * D], dtype=np.float32)
    wv_h = np.ascontiguousarray(
        Wv.reshape(DC, 128, 2, 1024).transpose(2, 0, 1, 3))
    wo_p = _pieces_st(np.asarray(inputs["L0_attn_a_W_out"]), D, D)
    ffnw_p = _pieces_st(np.ascontiguousarray(np.asarray(inputs["L0_ffn_weight"]).T),
                        D, D)
    wz_p = _pieces_st(np.asarray(inputs["L0_attn_b_Wz"]), D, MI)
    wx_p = _pieces_st(np.asarray(inputs["L0_attn_b_Wx"]), D, MI)
    wb_p = _pieces_st(np.asarray(inputs["L0_attn_b_WB"]), D, GN)
    wc_p = _pieces_st(np.asarray(inputs["L0_attn_b_WC"]), D, GN)
    Wdt = np.asarray(inputs["L0_attn_b_Wdt"], dtype=np.float32).reshape(DC, 128, 64)
    wdt_p = np.ascontiguousarray(
        Wdt.reshape(2, 8, 128, 64).transpose(0, 2, 1, 3).reshape(2, 128, 512))
    wom_p = _pieces_st(np.asarray(inputs["L0_attn_b_Wout"]), MI, D)

    def colchunks(v, n):
        return np.ascontiguousarray(np.asarray(v, dtype=np.float32).reshape(n, 128).T)

    ln_w_h = colchunks(inputs["L0_attn_a_ln_w"], DC)
    ln_b_h = colchunks(inputs["L0_attn_a_ln_b"], DC)
    ffnb_h = colchunks(inputs["L0_ffn_bias"], DC)
    normw_h = colchunks(inputs["L0_attn_b_norm_w"], 32)
    convb_h = colchunks(inputs["L0_attn_b_conv_b"], NCC)
    convw_h = np.ascontiguousarray(
        np.asarray(inputs["L0_attn_b_conv_w"], dtype=np.float32)
        .reshape(NCC, 128, 4).transpose(1, 0, 2))
    dtb_h = np.asarray(inputs["L0_attn_b_dt_bias"], np.float32).reshape(64, 1)
    alog_h = np.asarray(inputs["L0_attn_b_A_log"], np.float32).reshape(64, 1)
    dvec_h = np.asarray(inputs["L0_attn_b_D"], np.float32).reshape(64, 1)

    identity = np.eye(128, dtype=np.float32)
    trim = (np.arange(128)[:, None] <= np.arange(128)[None, :]).astype(np.float32)
    onesm_h = np.ones((128, 128), np.float32)
    onesc_h = np.ones((128, 1), np.float32)

    inv_freq = (1.0 / (10000.0 ** (np.arange(0, 64, 2, dtype=np.float32) / 64.0))
                ).astype(np.float32)

    in_maps = []
    for i in range(NCORES):
        pos = np.arange(i * SQ, (i + 1) * SQ, dtype=np.float32)
        ang = (pos[:, None] * inv_freq[None, :]).astype(np.float32)
        cosj = np.cos(ang).astype(np.float32).T
        sinj = np.sin(ang).astype(np.float32).T
        cos2 = np.ascontiguousarray(np.concatenate([cosj, cosj], 0))
        sinS = np.ascontiguousarray(np.concatenate([-sinj, sinj], 0))
        pm = np.zeros((1, 8), np.float32)
        pm[0, :i] = 1.0
        hs = np.zeros((1, 8), np.float32)
        if i > 0:
            hs[0, i - 1] = 1.0
        in_maps.append({
            "xT": np.ascontiguousarray(X[i * SQ:(i + 1) * SQ].T, dtype=np.float32),
            "wq": wq_p, "wk": wk_p, "wv": wv_h, "wo": wo_p,
            "ln_w": ln_w_h, "ln_b": ln_b_h, "ffnw": ffnw_p, "ffnb": ffnb_h,
            "wz": wz_p, "wxp": wx_p, "wbp": wb_p, "wcp": wc_p, "wdt": wdt_p,
            "wom": wom_p, "convw": convw_h, "convb": convb_h,
            "dtbias": dtb_h, "alog": alog_h, "dvec": dvec_h, "normw": normw_h,
            "ropec": cos2, "ropes": sinS, "prevmask": pm, "halosel": hs,
            "identc": identity, "trimask": trim, "onesmat": onesm_h,
            "onescol": onesc_h, "epscol": np.full((128, 1), EPS, np.float32),
            "zrow": np.zeros((1, MH * MD), np.float32),
            "haloselB": np.tile(hs, (128, 1)),
        })

    nc = build_program()
    res = run_bass_kernel_spmd(nc, in_maps, core_ids=list(range(NCORES)))
    out = np.empty((1, S, D), dtype=np.float32)
    for i in range(NCORES):
        out[0, i * SQ:(i + 1) * SQ] = res.results[i]["y"].T
    return out


# version 17
# speedup vs baseline: 1.0215x; 1.0215x over previous
import math
import numpy as np
import concourse.bass as bass
import concourse.mybir as mybir
from concourse.bass_utils import run_bass_kernel_spmd

F32 = mybir.dt.float32
F32R = mybir.dt.float32r
FP8 = mybir.dt.float8e4
ALU = mybir.AluOpType
ACTF = mybir.ActivationFunctionType
AXX = mybir.AxisListType.X

NCORES = 8
S, D = 8192, 2048
SQ = S // NCORES
DC = D // 128
NH = 16
QT = 512
NQT = 2
EPS = 1e-5
FP8M = 448.0 * 0.8
ISQ = 1.0 / math.sqrt(128.0)
MI, MH, MD = 4096, 64, 64
NG, NS = 8, 128
GN = NG * NS
CONV = MI + 2 * GN
NCC = CONV // 128
LC = 128
NTC = SQ // LC

N_SUBLAYERS = 4
DEBUG_DUMPS = False

ENGINES = ("sync", "gpsimd", "scalar", "vector", "tensor")


class Prog:

    def __init__(self, nc):
        self.nc = nc
        self.q = {e: [] for e in ENGINES}
        self.esem = {}
        self.ecnt = {}
        self.tag_eng = {}
        self.tag_hist = {}
        self.last_wait = {e: {} for e in ENGINES}

    def _key(self, eng, is_dma):
        k = f"{eng}_dma" if is_dma else eng
        if k not in self.esem:
            self.esem[k] = self.nc.alloc_semaphore(f"prog_{k}")
            self.ecnt[k] = 0
        return k

    def v(self, name):
        h = self.tag_hist.get(name)
        return len(h) if h else 0

    def _record(self, eng, name, inc, is_dma):
        k = self._key(eng, is_dma)
        assert self.tag_eng.setdefault(name, k) == k, name
        self.ecnt[k] += inc
        self.tag_hist.setdefault(name, []).append(self.ecnt[k])
        return len(self.tag_hist[name])

    def _waits(self, eng, wait):
        out = {}
        for s, val in wait:
            if val <= 0 or s not in self.tag_hist:
                continue
            hist = self.tag_hist[s]
            assert val <= len(hist), (s, val, len(hist))
            peng = self.tag_eng[s]
            a = hist[val - 1]
            if out.get(peng, -1) < a:
                out[peng] = a
        res = []
        for peng, a in out.items():
            if self.last_wait[eng].get(peng, -1) < a:
                self.last_wait[eng][peng] = a
                res.append((self.esem[peng], a))
        return res

    def op(self, eng, builder, sem=None, inc=1, wait=(), is_dma=False):
        ws = self._waits(eng, wait)
        h = None
        if sem is not None:
            self._record(eng, sem, inc, is_dma)
            h = self.esem[self._key(eng, is_dma)]

        def fn(E, builder=builder, ws=ws, h=h, inc=inc):
            for sh, val in ws:
                E.wait_ge(sh, val)
            inst = builder(E)
            if h is not None:
                inst.then_inc(h, inc)
        self.q[eng].append(fn)
        return self.v(sem) if sem else None

    def dma(self, eng, out, in_, sem=None, wait=()):
        if sem is None:
            sem = f"_dma_{eng}"
        return self.op(eng, lambda E, o=out, i=in_: E.dma_start(out=o, in_=i),
                       sem=sem, inc=16, wait=wait, is_dma=True)

    def rdma(self, eng, out, in_, sem=None, wait=()):
        return self.dma(eng, out.bitcast(F32R) if out.dtype != F32R else out,
                        in_.bitcast(F32R) if in_.dtype != F32R else in_,
                        sem=sem, wait=wait)

    def wait(self, eng, wait):
        ws = self._waits(eng, wait)
        if not ws:
            return

        def fn(E, ws=ws):
            for sh, val in ws:
                E.wait_ge(sh, val)
        self.q[eng].append(fn)

    def collective(self, kind, op, ins, outs, sem, wait=()):
        ws = self._waits("gpsimd", wait)
        self._record("gpsimd", sem, 1, False)
        h = self.esem[self._key("gpsimd", False)]

        def fn(E, ws=ws, h=h, kind=kind, op=op, ins=ins, outs=outs):
            for sh, val in ws:
                E.wait_ge(sh, val)
            E.collective_compute(kind, op, replica_groups=[list(range(NCORES))],
                                 ins=ins, outs=outs).then_inc(h, 1)
        self.q["gpsimd"].append(fn)
        return self.v(sem)


def build_program():
    nc = bass.Bass("TRN2", target_bir_lowering=False, dynamic_dma_scratch_size=1024,
                   detect_race_conditions=False)
    nc.allow_low_precision(reason="float32r outputs are 4-byte fp32 storage").__enter__()
    p = Prog(nc)

    def inp(name, shape):
        return nc.dram_tensor(name, list(shape), F32, kind="ExternalInput").ap()

    xin = inp("xT", (D, SQ))
    wq = inp("wq", (DC * 4, 128, 512))
    wk = inp("wk", (DC * 4, 128, 512))
    wv = inp("wv", (2, DC, 128, 1024))
    wo = inp("wo", (DC * 4, 128, 512))
    ln_w = inp("ln_w", (128, DC))
    ln_b = inp("ln_b", (128, DC))
    ffnw = inp("ffnw", (DC * 4, 128, 512))
    ffnb = inp("ffnb", (128, DC))
    wz = inp("wz", (32 * 4, 128, 512))
    wxp = inp("wxp", (32 * 4, 128, 512))
    wbp = inp("wbp", (8 * 4, 128, 512))
    wcp = inp("wcp", (8 * 4, 128, 512))
    wdt = inp("wdt", (2, 128, 512))
    wom = inp("wom", (DC * 8, 128, 512))
    convw = inp("convw", (128, NCC, 4))
    convb = inp("convb", (128, NCC))
    dtbias = inp("dtbias", (64, 1))
    alog = inp("alog", (64, 1))
    dvec = inp("dvec", (64, 1))
    normw = inp("normw", (128, MI // 128))
    ropec = inp("ropec", (64, SQ))
    ropesn = inp("ropes", (64, SQ))
    prevmask = inp("prevmask", (1, 8))
    halosel = inp("halosel", (1, 8))
    haloselB = inp("haloselB", (128, 8))
    identc = inp("identc", (128, 128))
    trimaskc = inp("trimask", (128, 128))
    onesmat = inp("onesmat", (128, 128))
    onescol = inp("onescol", (128, 1))
    epscol = inp("epscol", (128, 1))
    zrow = inp("zrow", (1, MH * MD))

    yout = nc.dram_tensor("y", [D, SQ], F32, kind="ExternalOutput").ap()

    kvb = nc.dram_tensor("kvb", [4097, SQ], F32)
    kvag = nc.dram_tensor("kvag", [NCORES * 4097, SQ], F32)
    kvcp = (nc.dram_tensor("kvcp", [NCORES * 4097, SQ], F32, kind="ExternalOutput")
            if DEBUG_DUMPS else None)
    qcp = (nc.dram_tensor("qcp", [128, 16384], F32, kind="ExternalOutput")
           if DEBUG_DUMPS else None)
    amxb = [nc.dram_tensor(f"amxb{i}", [1, 8], F32) for i in range(2)]
    amxo = [nc.dram_tensor(f"amxo{i}", [1, 8], F32) for i in range(2)]
    xspill = nc.dram_tensor("xspill", [128, DC * SQ], F32)
    bspill = nc.dram_tensor("bspill", [NG, 128, SQ], F32)
    btspill = nc.dram_tensor("btspill", [NG, NTC, 128, 128], F32)
    cspill = nc.dram_tensor("cspill", [NG, 128, SQ], F32)
    halob = nc.dram_tensor("halob", [128, DC * 3], F32)
    haloag = nc.dram_tensor("haloag", [NCORES * 128, DC * 3], F32)
    stb = nc.dram_tensor("stb", [NS + 1, MH * MD], F32)
    dkind = "ExternalOutput" if DEBUG_DUMPS else "Internal"
    dbg1 = nc.dram_tensor("dbg1", [128, 1536], F32, kind=dkind)
    dbg2 = nc.dram_tensor("dbg2", [128, 2048], F32, kind=dkind)
    dbg3 = nc.dram_tensor("dbg3", [128, DC * SQ], F32, kind=dkind)
    stag = nc.dram_tensor("stag", [NCORES * (NS + 1), MH * MD], F32)

    XR = nc.alloc_sbuf_tensor("XR", [128, DC * SQ], F32R)
    A1 = nc.alloc_sbuf_tensor("A1", [128, DC * SQ], F32R)
    A2 = nc.alloc_sbuf_tensor("A2", [128, DC * SQ], F32R)
    EX = nc.alloc_sbuf_tensor("EX", [128, 2048], F32R)
    WSB = nc.alloc_sbuf_tensor("WSB", [128, 1024], F32R)
    BCT = nc.alloc_sbuf_tensor("BCT", [128, 2048], F32)
    RT = nc.alloc_sbuf_tensor("RT", [128, 1280], F32R)
    CT = nc.alloc_sbuf_tensor("CT", [128, 384], F32R)
    SM = nc.alloc_sbuf_tensor("SM", [128, 328], F32)
    DBv = nc.alloc_sbuf_tensor("DBv", [128, 64], F32)
    CBt = nc.alloc_sbuf_tensor("CBt", [128, 16], F32)
    S1 = nc.alloc_sbuf_tensor("S1", [64, 8], F32R)
    R1 = nc.alloc_sbuf_tensor("R1", [1, 64], F32R)
    OCl = nc.alloc_sbuf_tensor("OCl", [128, 1], F32R)
    EPC = nc.alloc_sbuf_tensor("EPC", [128, 1], F32)
    QS = nc.alloc_sbuf_tensor("QS", [128, 1024], FP8)
    WS8 = nc.alloc_sbuf_tensor("WS8", [128, 512], FP8)
    HSL = nc.alloc_sbuf_tensor("HSL", [128, 8], F32)

    ident = CT[:, 0:128]
    tri = CT[:, 128:256]
    onesm = CT[:, 256:384]
    OC = OCl[:]

    SM_LNW, SM_LNB, SM_FFB, SM_NRMW, SM_CVB, SM_CVW, SM_SCL = 0, 16, 32, 48, 80, 128, 320

    PS = [nc.alloc_psum_tensor(f"ps{i}", [128, 512], F32) for i in range(8)]

    p.dma("sync", stb.ap()[NS:NS + 1, :], zrow, sem="xl")
    p.dma("sync", XR[:].rearrange("p (c t) -> p c t", t=SQ),
          xin.bitcast(F32R).rearrange("(c p) t -> p c t", p=128), sem="xl")
    for dst, src in ((ident, identc), (tri, trimaskc), (onesm, onesmat), (OC, onescol)):
        p.dma("sync", dst, src.bitcast(F32R), sem="xl")
    p.dma("sync", EPC[:], epscol, sem="xl")

    p.dma("sync", HSL[:], haloselB, sem="xl")
    p.dma("sync", SM[:, SM_LNW:SM_LNW + 16], ln_w, sem="xl")
    p.dma("sync", SM[:, SM_LNB:SM_LNB + 16], ln_b, sem="xl")
    p.dma("sync", SM[:, SM_FFB:SM_FFB + 16], ffnb, sem="xl")
    p.dma("sync", SM[:, SM_NRMW:SM_NRMW + 32], normw, sem="xl")
    p.dma("sync", SM[:, SM_CVB:SM_CVB + 48], convb, sem="xl")
    p.dma("sync", SM[:, SM_CVW:SM_CVW + 192].rearrange("p (c k) -> p c k", k=4),
          convw, sem="xl")
    p.dma("sync", S1[:, 0:1], dtbias.bitcast(F32R), sem="xl")
    p.dma("sync", S1[:, 1:2], alog.bitcast(F32R), sem="xl")
    p.dma("sync", S1[:, 2:3], dvec.bitcast(F32R), sem="xl")
    p.dma("sync", R1[0:1, 48:56], prevmask.bitcast(F32R), sem="xl")
    p.dma("sync", R1[0:1, 56:64], halosel.bitcast(F32R), sem="xl")
    XL = p.v("xl")

    def mmrow(E, bank, row_ap, n0, w):
        ap = row_ap[0:1, n0:n0 + w]
        if ap.dtype != F32:
            ap = ap.bitcast(F32)
        return E.matmul(bank[:, 0:w], onesm[0:1, 0:128].bitcast(F32),
                        ap, start=True, stop=True)

    aB = BCT[:, 0:1024]
    bB = BCT[:, 1024:2048]

    def ln_phase(sqdst, double, wait_extra=()):
        we = list(wait_extra)
        for c in range(DC):
            p.op("scalar", lambda E, c=c: E.activation(
                out=sqdst[:, c * SQ:(c + 1) * SQ], in_=XR[:, c * SQ:(c + 1) * SQ],
                func=ACTF.Square), sem="lsq",
                wait=(we + [("xl", XL)]) if c == 0 else ())
        sqv = p.v("lsq")

        def stats(E):
            last = None
            for c in range(DC):
                for ti in range(NQT):
                    sl = slice(ti * QT, (ti + 1) * QT)
                    last = E.matmul(PS[ti][0:1, :], OC,
                                    XR[:, c * SQ:(c + 1) * SQ][:, sl],
                                    start=(c == 0), stop=(c == DC - 1))
                    last = E.matmul(PS[2 + ti][0:1, :], OC,
                                    sqdst[:, c * SQ:(c + 1) * SQ][:, sl],
                                    start=(c == 0), stop=(c == DC - 1))
            return last
        p.op("tensor", stats, sem="lst", wait=we + [("lsq", sqv), ("xl", XL)])
        lstv = p.v("lst")
        mrow = EX[0:1, 0:1024]
        rv = BCT[0:1, 1024:2048]
        rt_ = BCT[0:1, 0:1024]
        RXR = RT[0:1, 0:1024]

        def rows1(E):
            last = None
            for ti in range(NQT):
                sl = slice(ti * QT, (ti + 1) * QT)
                E.tensor_scalar(out=mrow[0:1, sl], in0=PS[ti][0:1, :], scalar1=1.0 / D,
                                scalar2=None, op0=ALU.mult)
                E.tensor_scalar(out=rv[0:1, sl], in0=PS[2 + ti][0:1, :],
                                scalar1=1.0 / D, scalar2=None, op0=ALU.mult)
                E.tensor_tensor(out=rt_[0:1, sl], in0=mrow[0:1, sl].bitcast(F32),
                                in1=mrow[0:1, sl].bitcast(F32), op=ALU.mult)
                last = E.tensor_tensor(out=rv[0:1, sl], in0=rv[0:1, sl],
                                       in1=rt_[0:1, sl], op=ALU.subtract)
            return last
        p.op("vector", rows1, sem="lrm", wait=[("lst", lstv)])
        if double:
            p.op("vector", lambda E: E.tensor_copy(out=rt_, in_=rv),
                 sem="lvc", wait=[("lrm", p.v("lrm"))])
        p.op("scalar", lambda E: E.activation(out=rv, in_=rv, func=ACTF.Sqrt,
                                              bias=EPC[0:1, 0:1]),
             sem="lr0", wait=[("lvc", p.v("lvc")), ("lrm", p.v("lrm"))])
        p.op("vector", lambda E: E.reciprocal(out=rv, in_=rv),
             sem="lr1", wait=[("lr0", p.v("lr0"))])
        if double:
            def v2(E):
                E.tensor_tensor(out=rt_, in0=rt_, in1=rv, op=ALU.mult)
                return E.tensor_tensor(out=rt_, in0=rt_, in1=rv, op=ALU.mult)
            p.op("vector", v2, sem="lv2", wait=[("lr1", p.v("lr1"))])
            p.op("scalar", lambda E: E.activation(out=rt_, in_=rt_, func=ACTF.Sqrt,
                                                  bias=EPC[0:1, 0:1]),
                 sem="lq2", wait=[("lv2", p.v("lv2"))])
            p.op("vector", lambda E: E.reciprocal(out=rt_, in_=rt_),
                 sem="lr2", wait=[("lq2", p.v("lq2"))])
            p.op("vector", lambda E: E.tensor_tensor(out=rv, in0=rv, in1=rt_,
                                                     op=ALU.mult),
                 sem="lar", wait=[("lr2", p.v("lr2"))])
        else:
            p.op("vector", lambda E: E.tensor_copy(out=rv[0:1, 0:4], in_=rv[0:1, 0:4]),
                 sem="lar", wait=[("lr1", p.v("lr1"))])
        p.op("vector", lambda E: E.tensor_copy(out=RXR, in_=rv), sem="lax",
             wait=[("lar", p.v("lar"))])

        def bca(E):
            mmrow(E, PS[4], RXR, 0, 512)
            return mmrow(E, PS[5], RXR, 512, 512)
        p.op("tensor", bca, sem="lba", wait=[("lax", p.v("lax"))])

        def cpa(E):
            E.tensor_copy(out=aB[:, 0:512], in_=PS[4][:, :])
            return E.tensor_copy(out=aB[:, 512:1024], in_=PS[5][:, :])
        p.op("vector", cpa, sem="lca", wait=[("lba", p.v("lba"))])

        def betarow(E):
            E.tensor_tensor(out=rv, in0=rv, in1=mrow.bitcast(F32), op=ALU.mult)
            return E.tensor_copy(out=RXR, in_=rv)
        p.op("vector", betarow, sem="lbr", wait=[("lca", p.v("lca"))])

        def bcb(E):
            mmrow(E, PS[4], RXR, 0, 512)
            return mmrow(E, PS[5], RXR, 512, 512)
        p.op("tensor", bcb, sem="lbb", wait=[("lbr", p.v("lbr"))])

        def cpb(E):
            E.tensor_copy(out=bB[:, 0:512], in_=PS[4][:, :])
            return E.tensor_copy(out=bB[:, 512:1024], in_=PS[5][:, :])
        p.op("vector", cpb, sem="lnd", wait=[("lbb", p.v("lbb"))])
        return p.v("lnd")

    ATT_DONE = 0
    if N_SUBLAYERS >= 1:
        lnv = ln_phase(A2, double=True)
        for c in range(DC):
            mid = A2[:, c * SQ:(c + 1) * SQ]

            def v1(E, c=c, mid=mid):
                E.tensor_tensor(out=mid, in0=XR[:, c * SQ:(c + 1) * SQ],
                                in1=aB, op=ALU.mult)
                return E.tensor_tensor(out=mid, in0=mid.bitcast(F32), in1=bB,
                                       op=ALU.subtract)
            p.op("vector", v1, sem="xnv", wait=[("lnd", lnv)] if c == 0 else ())
            p.op("scalar", lambda E, c=c, mid=mid: E.activation(
                out=A1[:, c * SQ:(c + 1) * SQ], in_=mid.bitcast(F32), func=ACTF.Identity,
                bias=SM[:, SM_LNB + c:SM_LNB + c + 1],
                scale=SM[:, SM_LNW + c:SM_LNW + c + 1]),
                sem="xn", wait=[("xnv", p.v("xnv"))])
        XN = p.v("xn")

        vflat = kvb.ap().rearrange("r t -> (r t)")[2048 * SQ:4096 * SQ].rearrange(
            "(t f) -> t f", f=D)
        for half in range(2):
            p.rdma("sync", A2[:].rearrange("p (a b) -> p a b", b=1024),
                   wv[half].rearrange("a p b -> p a b"), sem="wvl",
                   wait=[("xn", XN)] if half == 0 else
                        [("vmm", p.v("vmm"))])
            WVL = p.v("wvl")
            for tc in range(NTC):
                for vt in range(2):
                    bank = PS[(tc * 2 + vt) % 2]

                    def vmm(E, tc=tc, vt=vt, bank=bank):
                        last = None
                        for dc in range(DC):
                            last = E.matmul(
                                bank[:, :],
                                A1[:, dc * SQ + tc * 128: dc * SQ + tc * 128 + 128],
                                A2[:, dc * 1024 + vt * 512: dc * 1024 + vt * 512 + 512],
                                start=(dc == 0), stop=(dc == DC - 1))
                        return last
                    p.op("tensor", vmm, sem="vmm",
                         wait=[("wvl", WVL), ("xn", XN), ("vev", p.v("vev") - 1)])
                    vslot = EX[:, (p.v("vmm") % 2) * 512:(p.v("vmm") % 2) * 512 + 512]
                    p.op("scalar", lambda E, bank=bank, vslot=vslot: E.activation(
                        out=vslot, in_=bank[:, :], func=ACTF.Copy), sem="vev",
                        wait=[("vmm", p.v("vmm")), ("vst", p.v("vst") - 1)])
                    p.dma("scalar",
                          vflat[tc * 128:(tc + 1) * 128,
                                half * 1024 + vt * 512: half * 1024 + (vt + 1) * 512],
                          vslot.bitcast(F32), sem="vst", wait=[("vev", p.v("vev"))])
        VST = p.v("vst")
        VMMALL = p.v("vmm")

        for i in range(64):
            slot = WSB[:, (i % 2) * 512:(i % 2) * 512 + 512]
            p.rdma("sync", slot, ffnw[i], sem="fal",
                   wait=[("far", p.v("far") - 1)])

            def red(E, slot=slot, i=i):
                E.tensor_reduce(out=RT[:, 1025:1026],
                                in_=slot.bitcast(F32), axis=AXX, op=ALU.max,
                                apply_absolute_value=True)
                if i == 0:
                    return E.tensor_copy(out=RT[:, 1024:1025],
                                         in_=RT[:, 1025:1026].bitcast(F32))
                return E.tensor_tensor(out=RT[:, 1024:1025],
                                       in0=RT[:, 1024:1025].bitcast(F32),
                                       in1=RT[:, 1025:1026].bitcast(F32), op=ALU.max)
            p.op("vector", red, sem="far", wait=[("fal", p.v("fal"))])
        FARALL = p.v("far")
        p.op("tensor", lambda E: E.transpose(
            PS[6][0:1, 0:128].bitcast(F32R),
            RT[:, 1024:1025], ident),
            sem="fat", wait=[("far", FARALL), ("xl", XL)])
        p.op("vector", lambda E: E.tensor_reduce(
            out=R1[0:1, 1:2], in_=PS[6][0:1, 0:128], axis=AXX, op=ALU.max),
            sem="faw", wait=[("fat", p.v("fat"))])

        p.dma("sync", BCT[0:64, 0:SQ], ropec, sem="rtl", wait=[("xnv", p.v("xnv"))])
        p.dma("sync", BCT[0:64, SQ:2 * SQ], ropesn, sem="rtl")
        RTL = p.v("rtl")
        cosT = BCT[0:64, 0:SQ]
        sinT = BCT[0:64, SQ:2 * SQ]

        kst = EX[:, 0:1024]
        ksw = EX[:, 1024:2048]

        def rope_ops(tag, buf, wait_first=()):
            p.dma("scalar", ksw[0:32, 0:1024], buf[32:64, :],
                  sem=f"{tag}s", wait=[(f"{tag}r", p.v(f"{tag}r"))] + list(wait_first))
            p.dma("scalar", ksw[32:64, 0:1024], buf[0:32, :], sem=f"{tag}s")
            sv = p.v(f"{tag}s")

            def rp(E, buf=buf):
                E.tensor_tensor(out=ksw[0:64, 0:1024], in0=ksw[0:64, 0:1024].bitcast(F32),
                                in1=sinT, op=ALU.mult)
                E.tensor_tensor(out=buf[0:64, :], in0=buf[0:64, :].bitcast(F32),
                                in1=cosT, op=ALU.mult)
                return E.tensor_tensor(out=buf[0:64, :], in0=buf[0:64, :].bitcast(F32),
                                       in1=ksw[0:64, 0:1024].bitcast(F32), op=ALU.add)
            p.op("vector", rp, sem=f"{tag}r", wait=[(f"{tag}s", sv), ("rtl", RTL)])
            return p.v(f"{tag}r")

        for fc in range(DC):
            for g in range(4):
                i = fc * 4 + g
                slot = WSB[:, (i % 2) * 512:(i % 2) * 512 + 512]
                p.rdma("sync", slot, wk[i], sem="kl",
                       wait=[("kc", i - 1)] + ([("far", FARALL)] if i < 2 else []))
                lv = p.v("kl")

                def mmk(E, g=g, slot=slot):
                    last = None
                    for j in range(4):
                        dc = g * 4 + j
                        for ti in range(NQT):
                            last = E.matmul(
                                PS[4 + ti][:, :], slot[:, j * 128:(j + 1) * 128],
                                A1[:, dc * SQ + ti * QT: dc * SQ + (ti + 1) * QT],
                                start=(dc == 0), stop=(dc == DC - 1))
                    return last
                p.op("tensor", mmk, sem="kc",
                     wait=[("kl", lv), ("xn", XN)] +
                          ([("kev", p.v("kev"))] if g == 0 else []))
            def evk(E):
                E.activation(out=kst[:, 0:512], in_=PS[4][:, :], func=ACTF.Copy)
                return E.activation(out=kst[:, 512:1024], in_=PS[5][:, :],
                                    func=ACTF.Copy)
            p.op("scalar", evk, sem="kev",
                 wait=[("kc", p.v("kc")), ("kdma", p.v("kdma")),
                       ("ksq", p.v("ksq")), ("vst", VST)])
            rv = rope_ops("k", kst, [("kev", p.v("kev")),
                                     ("knm", p.v("knm"))])
            p.op("scalar", lambda E: E.activation(
                out=ksw[:, 0:1024], in_=kst.bitcast(F32), func=ACTF.Square),
                sem="ksq", wait=[("kr", rv)])

            def knmm(E):
                E.matmul(PS[6][0:1, :], OC, ksw[:, 0:512], start=True, stop=True)
                return E.matmul(PS[7][0:1, :], OC, ksw[:, 512:1024],
                                start=True, stop=True)
            p.op("tensor", knmm, sem="knm", wait=[("ksq", p.v("ksq")),
                                                  ("knr", p.v("knr")),
                                                  ("faw", p.v("faw"))])

            def knred(E, fc=fc):
                E.tensor_reduce(out=R1[0:1, 32 + fc:33 + fc], in_=PS[6][0:1, :],
                                axis=AXX, op=ALU.max)
                E.tensor_reduce(out=R1[0:1, 2:3], in_=PS[7][0:1, :], axis=AXX,
                                op=ALU.max)
                return E.tensor_tensor(out=R1[0:1, 32 + fc:33 + fc],
                                       in0=R1[0:1, 32 + fc:33 + fc],
                                       in1=R1[0:1, 2:3], op=ALU.max)
            p.op("vector", knred, sem="knr", wait=[("knm", p.v("knm"))])
            p.dma("scalar", kvb.ap()[fc * 128:(fc + 1) * 128, :], kst.bitcast(F32),
                  sem="kdma", wait=[("kr", rv)])
        p.dma("sync", RT[0:1, 0:1024], zrow[0:1, 0:1024].bitcast(F32R),
              sem="knz", wait=[("knm", p.v("knm"))])
        def nrow(E):
            return E.tensor_copy(out=RT[0:1, 0:16], in_=R1[0:1, 32:48].bitcast(F32))
        p.op("vector", nrow, sem="knw", wait=[("knr", p.v("knr")),
                                              ("knz", p.v("knz"))])
        p.dma("scalar", kvb.ap()[4096:4097, 0:1024], RT[0:1, 0:1024].bitcast(F32), sem="kdma",
              wait=[("knw", p.v("knw"))])
        KD = p.v("kdma")

        ag1 = p.collective("AllGather", ALU.bypass, [kvb.ap()], [kvag.ap()],
                           sem="cc", wait=[("kdma", KD), ("vst", VST)])

        def emit_qload(i):
            if i < 64:
                p.rdma("sync", WSB[:, (i % 2) * 512:(i % 2) * 512 + 512], wq[i],
                       sem="ql", wait=[("qc", i - 1)] +
                                      ([("kc", 64)] if i < 2 else []))
            elif i == 64:
                p.rdma("sync", WSB[:, 0:512], wq[0], sem="ql",
                       wait=[("qc", 63)])
        emit_qload(0)
        for fc in range(DC):
            for g in range(4):
                i = fc * 4 + g
                slot = WSB[:, (i % 2) * 512:(i % 2) * 512 + 512]
                emit_qload(i + 1)
                lv = p.v("ql")

                def mmq(E, g=g, slot=slot):
                    last = None
                    for j in range(4):
                        dc = g * 4 + j
                        for ti in range(NQT):
                            last = E.matmul(
                                PS[4 + ti][:, :], slot[:, j * 128:(j + 1) * 128],
                                A1[:, dc * SQ + ti * QT: dc * SQ + (ti + 1) * QT],
                                start=(dc == 0), stop=(dc == DC - 1))
                    return last
                p.op("tensor", mmq, sem="qc",
                     wait=[("ql", lv)] + ([("qev", p.v("qev"))] if g == 0 else []))
            qch = A2[:, fc * SQ:(fc + 1) * SQ]

            def evq(E, qch=qch):
                E.activation(out=qch[:, 0:512], in_=PS[4][:, :], func=ACTF.Copy)
                return E.activation(out=qch[:, 512:1024], in_=PS[5][:, :],
                                    func=ACTF.Copy)
            p.op("scalar", evq, sem="qev",
                 wait=[("qc", p.v("qc")), ("wvl", p.v("wvl")), ("vmm", VMMALL)])
            rv = rope_ops("q", qch, [("qev", p.v("qev")),
                                     ("qnm", p.v("qnm")), ("knm", p.v("knm"))])
            p.op("scalar", lambda E, qch=qch: E.activation(
                out=ksw[:, 0:1024], in_=qch.bitcast(F32), func=ACTF.Square),
                sem="qsq", wait=[("qr", rv), ("ksq", p.v("ksq")),
                                 ("knm", p.v("knm")), ("qnm", p.v("qnm"))])

            def qnmm(E):
                E.matmul(PS[6][0:1, :], OC, ksw[:, 0:512], start=True, stop=True)
                return E.matmul(PS[7][0:1, :], OC, ksw[:, 512:1024],
                                start=True, stop=True)
            p.op("tensor", qnmm, sem="qnm", wait=[("qsq", p.v("qsq")),
                                                  ("qnr", p.v("qnr"))])

            def qnred(E, fc=fc):
                E.tensor_reduce(out=R1[0:1, 16 + fc:17 + fc], in_=PS[6][0:1, :],
                                axis=AXX, op=ALU.max)
                E.tensor_reduce(out=R1[0:1, 2:3], in_=PS[7][0:1, :], axis=AXX,
                                op=ALU.max)
                return E.tensor_tensor(out=R1[0:1, 16 + fc:17 + fc],
                                       in0=R1[0:1, 16 + fc:17 + fc],
                                       in1=R1[0:1, 2:3], op=ALU.max)
            p.op("vector", qnred, sem="qnr", wait=[("qnm", p.v("qnm"))])
        QNR = p.v("qnr")
        if DEBUG_DUMPS and N_SUBLAYERS == 1:
            p.dma("sync", qcp.ap(), A2[:].bitcast(F32), sem="dbgq",
                  wait=[("qr", p.v("qr"))])

        p.dma("sync", RT[0:1, 0:128].rearrange("a (j t) -> a j t", t=16),
              kvag.ap().rearrange("(j r) t -> j r t", r=4097)[:, 4096, 0:16]
              .rearrange("j t -> () j t").bitcast(F32R),
              sem="kng", wait=[("cc", ag1)])

        def cbrow(E):
            E.tensor_reduce(out=RT[0:1, 128:144],
                            in_=RT[0:1, 0:128].rearrange("a (j h) -> a h j", h=16),
                            axis=AXX, op=ALU.max)
            return E.tensor_tensor(out=RT[0:1, 128:144], in0=RT[0:1, 128:144],
                                   in1=R1[0:1, 16:32], op=ALU.mult)
        p.op("vector", cbrow, sem="cbv", wait=[("kng", p.v("kng")), ("qnr", QNR)])
        p.op("scalar", lambda E: E.activation(
            out=RT[0:1, 128:144], in_=RT[0:1, 128:144], func=ACTF.Sqrt),
            sem="cbs", wait=[("cbv", p.v("cbv"))])
        p.op("vector", lambda E: E.tensor_scalar(
            out=RT[0:1, 128:144], in0=RT[0:1, 128:144], scalar1=-ISQ, scalar2=None,
            op0=ALU.mult), sem="cbn", wait=[("cbs", p.v("cbs"))])
        p.op("tensor", lambda E: mmrow(E, PS[6], RT, 128, 16), sem="cbm",
             wait=[("cbn", p.v("cbn")), ("qnm", p.v("qnm"))])
        p.op("vector", lambda E: E.tensor_copy(out=CBt[:, :], in_=PS[6][:, 0:16]),
             sem="cbb", wait=[("cbm", p.v("cbm"))])
        CBB = p.v("cbb")

        kvagr = kvag.ap().rearrange("(j r) t -> j r t", r=4097)

        def kvbufs(itx):
            dbuf = itx >= 16 and itx % 2 == 1
            return ((A2[:, 0:1024], A2[:, 1024:2048]) if dbuf
                    else (EX[:, 0:1024], EX[:, 1024:2048]))

        def emit_kv(itx):
            if itx > NH * NCORES - 1:
                return
            h_, j_ = divmod(itx, NCORES)
            ksl_, vsl_ = kvbufs(itx)
            if itx < 16:
                ldw = [("sm", 8 * itx), ("pv", 8 * itx)]
            elif itx == 16:
                ldw = [("sm", 120), ("pv", 120)]
            elif itx == 17:
                ldw = [("sm", 128), ("pv", 128)]
            else:
                ldw = [("sm", 8 * (itx - 1)), ("pv", 8 * (itx - 1))]
            for hf in range(2):
                p.rdma("sync", ksl_[:, hf * 512:(hf + 1) * 512],
                       kvagr[j_, h_ * 128:(h_ + 1) * 128, hf * 512:(hf + 1) * 512],
                       sem="kvl",
                       wait=list(ldw) +
                            ([("qsq", p.v("qsq")), ("kr", p.v("kr")),
                              ("qr", p.v("qr")), ("qnm", p.v("qnm")),
                              ("knm", p.v("knm")), ("kdma", p.v("kdma")),
                              ("ksq", p.v("ksq"))] if itx == 0 else []))
                vfl = kvag.ap().rearrange("r t -> (r t)")[
                    (j_ * 4097 + 2048) * SQ:(j_ * 4097 + 4096) * SQ].rearrange(
                    "(t f) -> t f", f=D)
                p.rdma("sync",
                       vsl_[:, hf * 512:(hf + 1) * 512].rearrange(
                           "p (i f) -> p i f", f=128),
                       vfl[hf * 512:(hf + 1) * 512, h_ * 128:(h_ + 1) * 128]
                       .rearrange("(i p) f -> p i f", p=128),
                       sem="kvl", wait=())
        for h in range(NH):
            for j in range(NCORES):
                it = h * NCORES + j
                if it < 16:
                    emit_kv(it)
                elif it == 16:
                    emit_kv(16)
                    emit_kv(17)
                else:
                    emit_kv(it + 1)
                ksl, vsl = kvbufs(it)
                kvlv = 4 * (it + 1) + (1 if 16 <= it < NH * NCORES - 1 else 0)
                for c in range(8):
                    hf = c // 4
                    b0 = 2 * (c % 2)

                    def smm(E, h=h, c=c, hf=hf, b0=b0, ksl=ksl):
                        last = None
                        for ti in range(NQT):
                            last = E.matmul(
                                PS[b0 + ti][:, :],
                                ksl[:, hf * 512 + (c % 4) * 128:
                                    hf * 512 + (c % 4) * 128 + 128],
                                A2[:, h * SQ + ti * QT: h * SQ + (ti + 1) * QT],
                                start=True, stop=True)
                        return last
                    p.op("tensor", smm, sem="sm",
                         wait=[("kvl", kvlv),
                               ("pe", p.v("pe") - 2)])
                    smv = p.v("sm")
                    pb = (WSB[:, 0:1024] if c % 2 == 0
                          else A1[:, h * SQ:h * SQ + 1024])
                    for ti in range(NQT):
                        p.op("scalar", lambda E, b0=b0, ti=ti, h=h, pb=pb:
                             E.activation(
                                 out=pb[:, ti * 512:(ti + 1) * 512],
                                 in_=PS[b0 + ti][:, :],
                                 func=ACTF.Exp, bias=CBt[:, h:h + 1], scale=ISQ),
                             sem="pe",
                             wait=[("sm", smv), ("pv", p.v("pv") - 1),
                                   ("cbb", CBB)] +
                                  ([("qc", 64)] if it == 0 and c == 0 else []))
                    pev = p.v("pe")

                    def pvmm(E, h=h, j=j, c=c, hf=hf, pb=pb, vsl=vsl):
                        first = (j == 0 and c == 0)
                        last_ = (j == NCORES - 1 and c == 7)
                        vch = vsl[:, hf * 512 + (c % 4) * 128:
                                  hf * 512 + (c % 4) * 128 + 128]
                        for ti in range(NQT):
                            E.matmul(PS[4 + ti][:, :], vch,
                                     pb[:, ti * 512:(ti + 1) * 512],
                                     start=first, stop=last_)
                        last = None
                        for ti in range(NQT):
                            last = E.matmul(PS[6 + ti][0:1, :], OC,
                                            pb[:, ti * 512:(ti + 1) * 512],
                                            start=first, stop=last_)
                        return last
                    p.op("tensor", pvmm, sem="pv",
                         wait=[("pe", pev)] +
                              ([("hv", h)] if (j == 0 and c == 0) else []))
            pvv = p.v("pv")

            def sumev(E):
                srow = BCT[0:1, 1024:2048]
                E.tensor_copy(out=srow[0:1, 0:512], in_=PS[6][0:1, :])
                E.tensor_copy(out=srow[0:1, 512:1024], in_=PS[7][0:1, :])
                E.reciprocal(out=srow, in_=srow)
                return E.tensor_copy(out=RT[0:1, 0:1024], in_=srow)
            p.op("vector", sumev, sem="se", wait=[("pv", pvv)])

            def rbc(E):
                mmrow(E, PS[6], RT, 0, 512)
                return mmrow(E, PS[7], RT, 512, 512)
            p.op("tensor", rbc, sem="rb", wait=[("se", p.v("se"))])

            def ctxev(E, h=h):
                E.tensor_copy(out=BCT[:, 0:512], in_=PS[6][:, :])
                E.tensor_copy(out=BCT[:, 512:1024], in_=PS[7][:, :])
                E.tensor_tensor(out=A1[:, h * SQ: h * SQ + 512], in0=PS[4][:, :],
                                in1=BCT[:, 0:512], op=ALU.mult)
                return E.tensor_tensor(out=A1[:, h * SQ + 512: h * SQ + 1024],
                                       in0=PS[5][:, :], in1=BCT[:, 512:1024],
                                       op=ALU.mult)
            p.op("vector", ctxev, sem="hv", wait=[("rb", p.v("rb"))])
        HV = p.v("hv")

        lnv2 = ln_phase(A2, double=False, wait_extra=[("hv", HV)])
        for fc in range(DC):
            for g in range(4):
                i = fc * 4 + g
                slot = WSB[:, (i % 2) * 512:(i % 2) * 512 + 512]
                p.rdma("sync", slot, wo[i], sem="ol",
                       wait=[("oc", i - 1)] + ([("hv", HV)] if i < 2 else []) +
                            ([("dw0", p.v("dw0"))] if i == 2 and DEBUG_DUMPS and
                             N_SUBLAYERS == 1 else []))
                lv = p.v("ol")
                if DEBUG_DUMPS and N_SUBLAYERS == 1 and i == 0:
                    p.op("vector", lambda E: E.tensor_copy(
                        out=A2[:, 1024:1536], in_=WSB[:, 0:512].bitcast(F32)),
                        sem="dw0", wait=[("ol", lv)])

                def mmo(E, g=g, slot=slot, fc=fc):
                    last = None
                    for j in range(4):
                        dc = g * 4 + j
                        for ti in range(NQT):
                            last = E.matmul(
                                PS[(fc % 2) * 2 + ti][:, :],
                                slot[:, j * 128:(j + 1) * 128],
                                A1[:, dc * SQ + ti * QT: dc * SQ + (ti + 1) * QT],
                                start=(dc == 0), stop=(dc == DC - 1))
                    return last
                p.op("tensor", mmo, sem="oc",
                     wait=[("ol", lv), ("lnd", lnv2)] +
                          ([("orr", p.v("orr") - 1)] if g == 0 else []))

            if DEBUG_DUMPS and N_SUBLAYERS == 1 and fc == 0:
                def dcp(E):
                    E.tensor_copy(out=A2[:, 0:512], in_=PS[0][:, :])
                    return E.tensor_copy(out=A2[:, 512:1024], in_=PS[1][:, :])
                p.op("vector", dcp, sem="dcp", wait=[("oc", p.v("oc"))])

            def resid(E, fc=fc):
                exf = EX[:, 0:1024]
                exff = exf.bitcast(F32)
                xc = XR[:, fc * SQ:(fc + 1) * SQ]
                E.tensor_tensor(out=exf, in0=xc, in1=aB, op=ALU.mult)
                E.tensor_tensor(out=exf, in0=exff, in1=bB, op=ALU.subtract)
                E.tensor_tensor(out=exf, in0=exff, in1=xc.bitcast(F32), op=ALU.add)
                E.tensor_tensor(out=xc[:, 0:512], in0=exff[:, 0:512],
                                in1=PS[(fc % 2) * 2][:, :], op=ALU.add)
                return E.tensor_tensor(out=xc[:, 512:1024], in0=exff[:, 512:1024],
                                       in1=PS[(fc % 2) * 2 + 1][:, :], op=ALU.add)
            p.op("vector", resid, sem="orr", wait=[("oc", p.v("oc"))])
        ATT_DONE = p.v("orr")
        if DEBUG_DUMPS and N_SUBLAYERS == 1:
            p.dma("sync", kvcp.ap(), kvag.ap(), sem="dbg2x", wait=[("cc", ag1)])
            p.dma("sync", dbg1.ap()[:, 0:1536], A2[:, 0:1536].bitcast(F32),
                  sem="dbg", wait=[("orr", ATT_DONE), ("dcp", p.v("dcp")),
                                   ("dw0", p.v("dw0"))])
            p.dma("sync", dbg2.ap(), BCT[:, :], sem="dbg")
            p.dma("sync", dbg3.ap(), A1[:].bitcast(F32), sem="dbg")

    ffn_i = [0]

    def phase_ffn(prev_sem, prev_val):
        k = ffn_i[0]
        ffn_i[0] += 1
        t = f"f{k}"
        lnv = ln_phase(A2, double=False, wait_extra=[(prev_sem, prev_val)])
        for c in range(DC):
            yc = A1[:, c * SQ:(c + 1) * SQ]

            def app(E, c=c, yc=yc):
                E.tensor_tensor(out=yc, in0=XR[:, c * SQ:(c + 1) * SQ],
                                in1=aB, op=ALU.mult)
                E.tensor_tensor(out=yc, in0=yc.bitcast(F32), in1=bB, op=ALU.subtract)
                E.tensor_reduce(out=RT[:, 1025:1026], in_=yc.bitcast(F32),
                                axis=AXX, op=ALU.max, apply_absolute_value=True)
                if c == 0:
                    return E.tensor_copy(out=RT[:, 1024:1025],
                                         in_=RT[:, 1025:1026].bitcast(F32))
                return E.tensor_tensor(out=RT[:, 1024:1025],
                                       in0=RT[:, 1024:1025].bitcast(F32),
                                       in1=RT[:, 1025:1026].bitcast(F32), op=ALU.max)
            p.op("vector", app, sem=t + "ya", wait=[("lnd", lnv)] if c == 0 else ())
        YA = p.v(t + "ya")
        p.op("tensor", lambda E: E.transpose(
            PS[6][0:1, 0:128].bitcast(F32R),
            RT[:, 1024:1025], ident),
            sem=t + "tx", wait=[(t + "ya", YA)])

        def amrow(E):
            E.tensor_scalar(out=R1[0:1, 8:16], in0=R1[0:1, 48:56].bitcast(F32),
                            scalar1=0.0, scalar2=None, op0=ALU.mult)
            return E.tensor_reduce(out=R1[0:1, 8:9], in_=PS[6][0:1, 0:128], axis=AXX,
                                   op=ALU.max)
        p.op("vector", amrow, sem=t + "am", wait=[(t + "tx", p.v(t + "tx"))])
        p.dma("sync", amxb[k].ap(), R1[0:1, 8:16].bitcast(F32), sem=t + "ab",
              wait=[(t + "am", p.v(t + "am"))])
        p.dma("sync", R1[0:1, 24:32], amxb[k].ap().bitcast(F32R), sem=t + "ab",
              wait=())
        arv = p.collective("AllReduce", ALU.max, [amxb[k].ap()], [amxo[k].ap()],
                           sem="cc", wait=[(t + "ab", p.v(t + "ab"))])
        p.dma("sync", R1[0:1, 8:16], amxo[k].ap().bitcast(F32R), sem=t + "ag",
              wait=[("cc", arv)])
        p.dma("sync", R1[0:1, 24:32], amxo[k].ap().bitcast(F32R), sem=t + "ag",
              wait=())

        crow = BCT[0:1, 1024:1040]
        p.op("vector", lambda E, c=crow: E.tensor_tensor(
            out=c[0:1, 0:1], in0=R1[0:1, 8:9].bitcast(F32),
            in1=R1[0:1, 1:2].bitcast(F32), op=ALU.max), sem=t + "s1",
            wait=[(t + "ag", p.v(t + "ag")), ("faw", p.v("faw"))])

        def scl2(E, c=crow):
            E.tensor_tensor(out=c[0:1, 1:2], in0=R1[0:1, 8:9].bitcast(F32),
                            in1=c[0:1, 0:1], op=ALU.mult)
            E.tensor_copy(out=c[0:1, 2:3], in_=R1[0:1, 8:9].bitcast(F32))
            return E.reciprocal(out=c[0:1, 3:4], in_=c[0:1, 0:1])
        p.op("vector", scl2, sem=t + "s2", wait=[(t + "s1", p.v(t + "s1"))])

        def scl3(E, c=crow):
            E.tensor_scalar(out=c[0:1, 1:2], in0=c[0:1, 1:2],
                            scalar1=4.0 / (FP8M * FP8M), scalar2=None, op0=ALU.mult)
            E.reciprocal(out=c[0:1, 2:3], in_=c[0:1, 2:3])
            return E.tensor_copy(out=c[0:1, 4:5], in_=c[0:1, 3:4])
        p.op("vector", scl3, sem=t + "s3", wait=[(t + "s2", p.v(t + "s2"))])

        def scl4(E, c=crow):
            E.tensor_scalar(out=c[0:1, 2:3], in0=c[0:1, 2:3], scalar1=0.5 * FP8M,
                            scalar2=None, op0=ALU.mult)
            return E.tensor_scalar(out=c[0:1, 4:5], in0=c[0:1, 4:5],
                                   scalar1=0.5 * FP8M, scalar2=None, op0=ALU.mult)
        p.op("vector", scl4, sem=t + "s4", wait=[(t + "s3", p.v(t + "s3"))])

        def scl5(E, c=crow):
            E.tensor_copy(out=R1[0:1, 11:12], in_=c[0:1, 2:3])
            E.tensor_copy(out=R1[0:1, 12:13], in_=c[0:1, 4:5])
            return E.tensor_copy(out=R1[0:1, 10:11], in_=c[0:1, 1:2])
        p.op("vector", scl5, sem=t + "sc", wait=[(t + "s4", p.v(t + "s4"))])

        def sclb(E):
            mmrow(E, PS[6], R1, 11, 1)
            mmrow(E, PS[7], R1, 12, 1)
            return mmrow(E, PS[5], R1, 10, 1)
        p.op("tensor", sclb, sem=t + "sb", wait=[(t + "sc", p.v(t + "sc"))])

        def sclc(E):
            E.tensor_copy(out=SM[:, SM_SCL + 0:SM_SCL + 1], in_=PS[6][:, 0:1])
            E.tensor_copy(out=SM[:, SM_SCL + 1:SM_SCL + 2], in_=PS[7][:, 0:1])
            return E.tensor_copy(out=SM[:, SM_SCL + 2:SM_SCL + 3], in_=PS[5][:, 0:1])
        p.op("vector", sclc, sem=t + "sv", wait=[(t + "sb", p.v(t + "sb"))])
        SV = p.v(t + "sv")

        for c in range(DC):
            p.op("scalar", lambda E, c=c: E.activation(
                out=QS[:, 0:SQ], in_=A1[:, c * SQ:(c + 1) * SQ].bitcast(F32),
                func=ACTF.Identity, scale=SM[:, SM_SCL + 0:SM_SCL + 1]),
                sem=t + "xq", wait=[(t + "sv", SV), (t + "x8", p.v(t + "x8"))])
            p.op("vector", lambda E, c=c: E.tensor_copy(
                out=A1[:, c * SQ:(c + 1) * SQ], in_=QS[:, 0:SQ]),
                sem=t + "x8", wait=[(t + "xq", p.v(t + "xq"))])
        X8 = p.v(t + "x8")
        x8f = A1
        w8f = [EX[:, 1024:1536], EX[:, 1536:2048]]
        for oc in range(DC):
            bsel = (oc % 2) * 2
            for g in range(4):
                i = oc * 4 + g
                slot = WSB[:, (i % 2) * 512:(i % 2) * 512 + 512]
                p.rdma("sync", slot, ffnw[i], sem=t + "wl",
                       wait=[(t + "wq", i - 1)] +
                            ([(prev_sem, prev_val), (t + "sv", SV)] if i < 2 else []))
                p.op("scalar", lambda E, slot=slot: E.activation(
                    out=WS8[:, 0:512], in_=slot.bitcast(F32), func=ACTF.Identity,
                    scale=SM[:, SM_SCL + 1:SM_SCL + 2]),
                    sem=t + "wq",
                    wait=[(t + "wl", p.v(t + "wl")), (t + "w8", p.v(t + "w8"))])
                p.op("vector", lambda E, i=i: E.tensor_copy(
                    out=w8f[i % 2], in_=WS8[:, 0:512]),
                    sem=t + "w8",
                    wait=[(t + "wq", p.v(t + "wq")), (t + "mm", p.v(t + "mm") - 1)])

                def fmm(E, oc=oc, g=g, bsel=bsel, i=i):
                    last = None
                    for j in range(4):
                        dc = g * 4 + j
                        for ti in range(NQT):
                            last = E.matmul(
                                PS[bsel + ti][:, :],
                                w8f[i % 2][:, j * 128:(j + 1) * 128],
                                x8f[:, dc * SQ + ti * QT: dc * SQ + (ti + 1) * QT],
                                start=(dc == 0), stop=(dc == DC - 1))
                    return last
                p.op("tensor", fmm, sem=t + "mm",
                     wait=[(t + "w8", p.v(t + "w8")), (t + "x8", X8)] +
                          ([(t + "ep", p.v(t + "ep") - 1)] if (oc >= 2 and g == 0) else []))

            def fep(E, oc=oc, bsel=bsel):
                E.activation(out=EX[:, 0:512], in_=PS[bsel][:, :], func=ACTF.Identity,
                             bias=SM[:, SM_FFB + oc:SM_FFB + oc + 1],
                             scale=SM[:, SM_SCL + 2:SM_SCL + 3])
                return E.activation(out=EX[:, 512:1024], in_=PS[bsel + 1][:, :],
                                    func=ACTF.Identity,
                                    bias=SM[:, SM_FFB + oc:SM_FFB + oc + 1],
                                    scale=SM[:, SM_SCL + 2:SM_SCL + 3])
            p.op("scalar", fep, sem=t + "ev",
                 wait=[(t + "mm", p.v(t + "mm")), (t + "ep", p.v(t + "ep") - 1)])

            def fadd(E, oc=oc):
                xc = XR[:, oc * SQ:(oc + 1) * SQ]
                return E.tensor_tensor(out=xc, in0=xc.bitcast(F32),
                                       in1=EX[:, 0:1024].bitcast(F32), op=ALU.add)
            p.op("vector", fadd, sem=t + "ep", wait=[(t + "ev", p.v(t + "ev"))])
        return p.v(t + "ep")

    if N_SUBLAYERS >= 2:
        FFN1_DONE = phase_ffn("orr", ATT_DONE)
        if DEBUG_DUMPS and N_SUBLAYERS == 2:
            p.dma("sync", dbg1.ap()[0:1, 0:64], R1[0:1, :].bitcast(F32),
                  sem="dbg", wait=[("f0ep", FFN1_DONE)])
            p.dma("sync", dbg2.ap()[:, 0:328], SM[:, :], sem="dbg")
            p.dma("sync", dbg3.ap(), A1[:].bitcast(F32), sem="dbg")
            p.dma("sync", dbg2.ap()[:, 1000:1016], BCT[:, 1024:1040], sem="dbg")

    if N_SUBLAYERS >= 3:
        p.dma("scalar", xspill.ap(), XR[:].bitcast(F32), sem="msp",
              wait=[("f0ep", FFN1_DONE)])
        MSP = p.v("msp")
        lnv = ln_phase(A2, double=False, wait_extra=[("f0ep", FFN1_DONE)])
        for c in range(DC):
            def app(E, c=c):
                xc = XR[:, c * SQ:(c + 1) * SQ]
                E.tensor_tensor(out=xc, in0=xc.bitcast(F32), in1=aB, op=ALU.mult)
                return E.tensor_tensor(out=xc, in0=xc.bitcast(F32), in1=bB,
                                       op=ALU.subtract)
            p.op("vector", app, sem="mh",
                 wait=[("lnd", lnv), ("msp", MSP)] if c == 0 else ())
        MH_ = p.v("mh")
        p.dma("sync", halob.ap().rearrange("p (c k) -> p c k", k=3),
              XR[:].bitcast(F32).rearrange("p (c t) -> p c t", t=SQ)[:, :, 1021:1024],
              sem="mhb", wait=[("mh", MH_)])
        agh = p.collective("AllGather", ALU.bypass, [halob.ap()], [haloag.ap()],
                           sem="cc", wait=[("mhb", p.v("mhb"))])

        for i in range(2):
            slot = WSB[:, i * 512:(i + 1) * 512]
            p.rdma("sync", slot, wdt[i], sem="dtl",
                   wait=[("mh", MH_), ("f0wq", p.v("f0wq"))])
        DTL = p.v("dtl")

        def dtmm(E):
            last = None
            for dc in range(DC):
                st = WSB[:, (dc // 8) * 512 + (dc % 8) * 64:
                         (dc // 8) * 512 + (dc % 8) * 64 + 64]
                for ti in range(NQT):
                    last = E.matmul(PS[ti][0:64, :], st,
                                    XR[:, dc * SQ + ti * QT: dc * SQ + (ti + 1) * QT],
                                    start=(dc == 0), stop=(dc == DC - 1))
            return last
        p.op("tensor", dtmm, sem="dtm", wait=[("dtl", DTL), ("mh", MH_)])

        def dtsp_a(E):
            E.activation(out=RT[0:64, 0:512], in_=PS[0][0:64, :], func=ACTF.Identity,
                         bias=S1[:, 0:1].bitcast(F32))
            E.activation(out=RT[0:64, 512:1024], in_=PS[1][0:64, :],
                         func=ACTF.Identity, bias=S1[:, 0:1].bitcast(F32))
            E.activation(out=EX[0:64, 0:1024], in_=RT[0:64, 0:1024].bitcast(F32R),
                         func=ACTF.Abs)
            E.activation(out=EX[0:64, 0:1024], in_=EX[0:64, 0:1024], func=ACTF.Exp,
                         scale=-1.0)
            E.activation(out=EX[0:64, 0:1024], in_=EX[0:64, 0:1024], func=ACTF.Ln,
                         bias=OCl[0:64, 0:1].bitcast(F32))
            return E.activation(out=RT[0:64, 0:1024].bitcast(F32R),
                                in_=RT[0:64, 0:1024].bitcast(F32R), func=ACTF.Relu)
        p.op("scalar", dtsp_a, sem="dtsa", wait=[("dtm", p.v("dtm"))])
        p.op("vector", lambda E: E.tensor_tensor(
            out=RT[0:64, 0:1024], in0=RT[0:64, 0:1024],
            in1=EX[0:64, 0:1024].bitcast(F32), op=ALU.add), sem="dts",
            wait=[("dtsa", p.v("dtsa"))])
        p.op("scalar", lambda E: E.activation(out=S1[:, 3:4], in_=S1[:, 1:2],
                                              func=ACTF.Exp), sem="dta",
             wait=[("xl", XL)])
        p.op("vector", lambda E: E.tensor_scalar(
            out=S1[:, 3:4], in0=S1[:, 3:4].bitcast(F32), scalar1=-1.0, scalar2=None, op0=ALU.mult),
            sem="dtn", wait=[("dta", p.v("dta"))])
        for tc in range(NTC):
            p.op("tensor", lambda E, tc=tc: E.transpose(
                PS[2][:, 0:64].bitcast(F32R),
                RT[0:64, tc * 128:(tc + 1) * 128].bitcast(F32R), ident[0:64, 0:64]),
                sem="dtt", wait=[("dts", p.v("dts")), ("dtv", p.v("dtv"))])
            p.op("scalar", lambda E, tc=tc: E.activation(
                out=EX[:, 1536 + tc * 64:1536 + (tc + 1) * 64], in_=PS[2][:, 0:64],
                func=ACTF.Copy), sem="dtv", wait=[("dtt", p.v("dtt"))])
        p.op("vector", lambda E: E.tensor_scalar(
            out=RT[0:64, 0:1024], in0=RT[0:64, 0:1024], scalar1=S1[:, 3:4].bitcast(F32),
            scalar2=None, op0=ALU.mult), sem="dak",
            wait=[("dtv", p.v("dtv")), ("dtn", p.v("dtn"))])
        for tc in range(NTC):
            p.op("tensor", lambda E, tc=tc: E.transpose(
                PS[2][:, 0:64].bitcast(F32R),
                RT[0:64, tc * 128:(tc + 1) * 128].bitcast(F32R), ident[0:64, 0:64]),
                sem="dat", wait=[("dak", p.v("dak")), ("dav", p.v("dav"))])
            p.op("scalar", lambda E, tc=tc: E.activation(
                out=EX[:, 512 + tc * 64:512 + (tc + 1) * 64],
                in_=PS[2][:, 0:64], func=ACTF.Copy), sem="dav",
                wait=[("dat", p.v("dat"))])
        DAV = p.v("dav")
        for tc in range(NTC):
            p.op("tensor", lambda E, tc=tc: E.matmul(
                PS[3][:, 0:64], tri,
                EX[:, 512 + tc * 64:512 + (tc + 1) * 64],
                start=True, stop=True), sem="stm",
                wait=[("dav", DAV), ("stv", p.v("stv"))])
            p.op("scalar", lambda E, tc=tc: E.activation(
                out=RT[:, tc * 64:(tc + 1) * 64].bitcast(F32R), in_=PS[3][:, 0:64],
                func=ACTF.Copy), sem="stv", wait=[("stm", p.v("stm"))])
        STV = p.v("stv")
        p.op("vector", lambda E: E.tensor_scalar(
            out=RT[:, 512:1024].bitcast(F32R), in0=RT[:, 0:512], scalar1=-1.0,
            scalar2=None, op0=ALU.mult), sem="nst", wait=[("stv", STV)])
        NSTV = p.v("nst")
        ST_ = RT[:, 0:512]
        nST = RT[:, 512:1024]
        dtT = EX[:, 1536:2048]
        p.op("vector", lambda E: E.tensor_copy(out=BCT[:, 0:512],
                                               in_=EX[:, 512:1024].bitcast(F32)),
             sem="dtp", wait=[("dav", DAV), ("lnd", lnv)])
        p.op("vector", lambda E: E.tensor_scalar(
            out=BCT[:, 512:1024], in0=EX[:, 512:1024].bitcast(F32), scalar1=-1.0,
            scalar2=None, op0=ALU.mult), sem="dtp", wait=())
        DTP = p.v("dtp")
        dtAT = BCT[:, 0:512]
        ndtAT = BCT[:, 512:1024]
        p.op("tensor", lambda E: E.matmul(PS[3][0:1, :], OC,
                                          EX[:, 512:1024], start=True, stop=True),
             sem="slm", wait=[("dav", DAV), ("stm", p.v("stm")),
                              ("stv", p.v("stv"))])
        p.op("vector", lambda E: E.tensor_copy(out=WSB[0:1, 64:576],
                                               in_=PS[3][0:1, :]), sem="sld",
             wait=[("slm", p.v("slm")), ("dtm", p.v("dtm"))])
        SLD = p.v("sld")
        slrow = WSB[0:1, 64:576]
        exrow = EX[0:1, 0:512]

        p.op("vector", lambda E: E.tensor_scalar(
            out=exrow[0:1, 0:64], in0=slrow[0:1, 0:64].bitcast(F32),
            scalar1=0.0, scalar2=None, op0=ALU.mult), sem="rfs",
            wait=[("sld", SLD)])
        for tc in range(1, NTC):
            p.op("vector", lambda E, tc=tc: E.tensor_tensor(
                out=exrow[0:1, tc * 64:(tc + 1) * 64],
                in0=exrow[0:1, (tc - 1) * 64:tc * 64].bitcast(F32),
                in1=slrow[0:1, (tc - 1) * 64:tc * 64].bitcast(F32),
                op=ALU.add), sem="rfs", wait=[("rfs", p.v("rfs"))])
        p.op("vector", lambda E: E.tensor_tensor(
            out=WSB[0:1, 576:640], in0=exrow[0:1, 448:512].bitcast(F32),
            in1=slrow[0:1, 448:512].bitcast(F32), op=ALU.add),
            sem="rof", wait=[("rfs", p.v("rfs"))])
        ROF = p.v("rof")
        p.dma("scalar", stb.ap()[NS:NS + 1, 0:64], WSB[0:1, 576:640].bitcast(F32),
              sem="lgd", wait=[("rof", ROF)])

        def slbm(E):
            mmrow(E, PS[4], slrow, 0, 512)
            return mmrow(E, PS[5], exrow, 0, 512)
        p.op("tensor", slbm, sem="slb", wait=[("rof", ROF)])

        def slbc(E):
            E.tensor_copy(out=BCT[:, 1536:2048], in_=PS[4][:, :])
            return E.tensor_copy(out=BCT[:, 1024:1536], in_=PS[5][:, :])
        p.op("vector", slbc, sem="slc", wait=[("slb", p.v("slb"))])
        SLC = p.v("slc")
        SLB = BCT[:, 1536:2048]
        RoffB = BCT[:, 1024:1536]

        p.op("tensor", lambda E: E.transpose(
            PS[4][0:1, 0:64].bitcast(F32R), S1[:, 2:3],
            ident[0:64, 0:64]), sem="dbm", wait=[("slc", SLC)])
        p.op("scalar", lambda E: E.activation(
            out=WSB[0:1, 640:704], in_=PS[4][0:1, 0:64], func=ACTF.Copy),
            sem="dbv", wait=[("dbm", p.v("dbm"))])
        p.op("tensor", lambda E: mmrow(E, PS[4], WSB[0:1, 640:704].bitcast(F32), 0, 64),
             sem="dbb", wait=[("dbv", p.v("dbv"))])
        p.op("vector", lambda E: E.tensor_copy(out=DBv[:, :], in_=PS[4][:, 0:64]),
             sem="dbc", wait=[("dbb", p.v("dbb"))])
        DBC = p.v("dbc")

        for j_ in range(NCORES):
            p.dma("sync", EX[:, 1027 + j_ * 48:1027 + (j_ + 1) * 48],
                  haloag.ap()[j_ * 128:(j_ + 1) * 128, :].bitcast(F32R),
                  sem="hld", wait=[("cc", agh)] if j_ == 0 else ())

        def hmask(E):
            last = None
            for j in range(NCORES):
                src = EX[:, 1027 + j * 48:1027 + j * 48 + 48].bitcast(F32)
                if j == 0:
                    last = E.tensor_scalar(out=EX[:, 1411:1459], in0=src,
                                           scalar1=HSL[:, j:j + 1],
                                           scalar2=None, op0=ALU.mult)
                else:
                    last = E.scalar_tensor_tensor(
                        out=EX[:, 1411:1459], in0=src,
                        scalar=HSL[:, j:j + 1],
                        in1=EX[:, 1411:1459].bitcast(F32),
                        op0=ALU.mult, op1=ALU.add)
            return last
        p.op("vector", hmask, sem="hmk", wait=[("hld", p.v("hld"))])
        HMK = p.v("hmk")
        haloh = EX[:, 1411:1459]

        fc_list = [("b", g, wbp, g * 4) for g in range(NG)] + \
                  [("c", g, wcp, g * 4) for g in range(NG)] + \
                  [("x", fc, wxp, fc * 4) for fc in range(32)]
        convsrc = EX[:, 0:1027]

        def x_of(fc):
            return (A1 if fc < DC else A2)[:, (fc % DC) * SQ:(fc % DC + 1) * SQ]

        for fi, (kind, idx, wsrc, p0) in enumerate(fc_list):
            cch = {"b": 32 + idx, "c": 40 + idx, "x": idx}[kind]
            for g in range(4):
                i = fi * 4 + g
                slot = WSB[:, (i % 2) * 512:(i % 2) * 512 + 512]
                p.rdma("sync", slot, wsrc[p0 + g], sem="pjl",
                       wait=[("pjc", i - 1)] +
                            ([("rof", ROF), ("dbc", DBC)] if i < 2 else []))
                lv = p.v("pjl")

                def mmp(E, g=g, slot=slot, fi=fi):
                    last = None
                    for j in range(4):
                        dc = g * 4 + j
                        for ti in range(NQT):
                            E.matmul(PS[(fi % 2) * 2 + ti][:, :],
                                     slot[:, j * 128:(j + 1) * 128],
                                     XR[:, dc * SQ + ti * QT: dc * SQ + (ti + 1) * QT],
                                     start=(dc == 0), stop=(dc == DC - 1))
                        last = E.matmul(PS[4 + (fi % 2)][:, 0:3],
                                        slot[:, j * 128:(j + 1) * 128].bitcast(F32),
                                        haloh[:, dc * 3:dc * 3 + 3].bitcast(F32),
                                        start=(dc == 0), stop=(dc == DC - 1))
                    return last
                p.op("tensor", mmp, sem="pjc",
                     wait=[("pjl", lv), ("hmk", HMK), ("mh", MH_)] +
                          ([("pcv", p.v("pcv") - 1)] if g == 0 else []))

            def pev(E, fi=fi):
                E.activation(out=convsrc[:, 0:3], in_=PS[4 + (fi % 2)][:, 0:3],
                             func=ACTF.Copy)
                E.activation(out=convsrc[:, 3:515], in_=PS[(fi % 2) * 2][:, :],
                             func=ACTF.Copy)
                return E.activation(out=convsrc[:, 515:1027],
                                    in_=PS[(fi % 2) * 2 + 1][:, :], func=ACTF.Copy)
            p.op("scalar", pev, sem="pje",
                 wait=[("pjc", p.v("pjc")), ("pcv", p.v("pcv"))])
            PJE = p.v("pje")
            if kind == "x":
                accf = x_of(idx)
                dst = x_of(idx)
            else:
                accf = A1[:, 0:1024]
                dst = A1[:, 0:1024]
            cs = convsrc.bitcast(F32)

            def cnv(E, accf=accf, cch=cch, cs=cs):
                E.tensor_scalar(out=accf, in0=cs[:, 0:1024],
                                scalar1=SM[:, SM_CVW + cch * 4:SM_CVW + cch * 4 + 1],
                                scalar2=SM[:, SM_CVB + cch:SM_CVB + cch + 1],
                                op0=ALU.mult, op1=ALU.add)
                last = None
                for kk in range(1, 4):
                    last = E.scalar_tensor_tensor(
                        out=accf, in0=cs[:, kk:kk + 1024],
                        scalar=SM[:, SM_CVW + cch * 4 + kk:SM_CVW + cch * 4 + kk + 1],
                        in1=accf.bitcast(F32), op0=ALU.mult, op1=ALU.add)
                return last
            p.op("vector", cnv, sem="pcv",
                 wait=[("pje", PJE), ("bcd", p.v("bcd")), ("btv", p.v("btv"))])
            p.op("scalar", lambda E, accf=accf, dst=dst: E.activation(
                out=dst, in_=accf.bitcast(F32), func=ACTF.Silu), sem="psl",
                wait=[("pcv", p.v("pcv"))])
            PSL = p.v("psl")
            if kind in ("b", "c"):
                sp = bspill if kind == "b" else cspill
                p.dma("scalar", sp.ap()[idx], A1[:, 0:1024].bitcast(F32), sem="bcd",
                      wait=[("psl", PSL)])
                if kind == "b":
                    for tc in range(NTC):
                        p.op("tensor", lambda E, tc=tc: E.transpose(
                            PS[6][:, 0:128].bitcast(F32R),
                            A1[:, tc * 128:(tc + 1) * 128], ident),
                            sem="btm", wait=[("psl", PSL), ("btv", p.v("btv"))])
                        p.op("scalar", lambda E: E.activation(
                            out=A1[:, 1024:1152], in_=PS[6][:, 0:128],
                            func=ACTF.Copy), sem="btv",
                            wait=[("btm", p.v("btm")), ("bcd", p.v("bcd"))])
                        p.dma("scalar", btspill.ap()[idx, tc],
                              A1[:, 1024:1152].bitcast(F32), sem="bcd",
                              wait=[("btv", p.v("btv"))])
        PJALL = p.v("psl")
        BCD = p.v("bcd")

        if DEBUG_DUMPS:
            p.dma("sync", dbg1.ap()[:, 0:1024], RT[:, 0:1024], sem="dbg",
                  wait=[("nst", NSTV), ("bcd", p.v("bcd")), ("slc", SLC)])
            p.dma("sync", dbg2.ap(), BCT[:, :], sem="dbg")
            p.dma("sync", dbg3.ap()[:, 1536:2048], EX[:, 1536:2048].bitcast(F32),
                  sem="dbg")
        states = EX[:, 0:1024]
        for ps_ in range(4):
            hs0 = ps_ * 16
            p.op("vector", lambda E: E.tensor_scalar(
                out=states[:], in0=A1[:, 0:1024].bitcast(F32), scalar1=0.0,
                scalar2=None, op0=ALU.mult), sem="sms",
                 wait=[("bcd", BCD), ("pcv", p.v("pcv")), ("std", p.v("std")),
                       ("slv", p.v("slv"))])
            SMS = p.v("sms")
            for tc in range(NTC):
                for hh in range(16):
                    h = hs0 + hh
                    g = h // 8
                    fc = h // 2
                    hl = (h % 2) * 64
                    sidx = hh * 64
                    xch = x_of(fc)
                    col = tc * 64 + h
                    if h % 8 == 0:
                        p.rdma("sync", RT[:, 1024:1152],
                               bspill.ap()[g, :, tc * 128:(tc + 1) * 128], sem="sbl",
                               wait=[("gcn", p.v("gcn"))])
                        p.rdma("sync", WSB[:, 128:256], btspill.ap()[g, tc],
                               sem="sbl", wait=[("bpp", p.v("bpp"))])
                        p.rdma("sync", RT[:, 1152:1280],
                               cspill.ap()[g, :, tc * 128:(tc + 1) * 128], sem="sbl",
                               wait=[("ccl", p.v("ccl"))])
                        sblv = p.v("sbl")
                        p.op("tensor", lambda E: E.matmul(
                            PS[0][:, 0:128], RT[:, 1024:1152].bitcast(F32R),
                            RT[:, 1152:1280].bitcast(F32R), start=True, stop=True),
                            sem="gmm", wait=[("sbl", sblv), ("sms", SMS),
                                             ("gcn", p.v("gcn"))])
                        p.op("vector", lambda E: E.tensor_tensor(
                            out=WSB[:, 0:128], in0=PS[0][:, 0:128],
                            in1=tri.bitcast(F32), op=ALU.mult), sem="gcn",
                            wait=[("gmm", p.v("gmm")), ("wcl", p.v("wcl"))])
                    GCN = p.v("gcn")
                    if h % 2 == 0:
                        p.op("tensor", lambda E, xch=xch, tc=tc: E.transpose(
                            PS[4][:, 0:128].bitcast(F32R),
                            xch[:, tc * 128:(tc + 1) * 128], ident),
                            sem="xtm", wait=[("psl", PJALL), ("xtv", p.v("xtv")),
                                             ("sms", SMS), ("dmm", p.v("dmm")),
                                             ("ymm", p.v("ymm"))])
                        for hj in range(2):
                            p.op("scalar", lambda E, hj=hj, col=col: E.activation(
                                out=WSB[:, 256 + hj * 64:256 + (hj + 1) * 64],
                                in_=PS[4][:, hj * 64:(hj + 1) * 64], func=ACTF.Identity,
                                scale=dtT[:, col + hj:col + hj + 1].bitcast(F32)),
                                sem="xtv", wait=[("xtm", p.v("xtm")),
                                                 ("dtv", p.v("dtv"))])
                    XTV = p.v("xtv")
                    p.op("vector", lambda E, col=col: E.tensor_scalar(
                        out=WSB[:, 768:896], in0=onesm.bitcast(F32),
                        scalar1=dtAT[:, col:col + 1], scalar2=None, op0=ALU.mult),
                        sem="w1b", wait=[("dtp", DTP), ("ccl", p.v("ccl"))])
                    p.op("vector", lambda E, col=col: E.tensor_scalar(
                        out=WSB[:, 640:768], in0=tri.bitcast(F32),
                        scalar1=ndtAT[:, col:col + 1], scalar2=None, op0=ALU.mult),
                        sem="w2b", wait=[("dtp", DTP), ("ccl", p.v("ccl"))])
                    p.op("tensor", lambda E: (
                        E.matmul(PS[1][:, 0:128], WSB[:, 768:896], tri,
                                 start=True, stop=False),
                        E.matmul(PS[1][:, 0:128], WSB[:, 640:768], onesm,
                                 start=False, stop=True),
                        E.matmul(PS[5][:, 0:128], WSB[:, 768:896], tri,
                                 start=True, stop=True))[-1], sem="sbm",
                        wait=[("w1b", p.v("w1b")), ("w2b", p.v("w2b")),
                              ("ecl", p.v("ecl"))])
                    p.op("vector", lambda E: E.tensor_tensor(
                        out=WSB[:, 384:512], in0=PS[1][:, 0:128],
                        in1=tri.bitcast(F32), op=ALU.mult), sem="eem",
                        wait=[("sbm", p.v("sbm")), ("dmm", p.v("dmm"))])
                    p.op("scalar", lambda E: E.activation(
                        out=WSB[:, 384:512], in_=WSB[:, 384:512].bitcast(F32R),
                        func=ACTF.Exp), sem="eex",
                        wait=[("eem", p.v("eem"))])
                    p.op("scalar", lambda E: E.activation(
                        out=WSB[:, 640:768], in_=PS[5][:, 0:128], func=ACTF.Exp),
                        sem="ecl", wait=[("eex", p.v("eex")), ("w2b", p.v("w2b")),
                                         ("ccl", p.v("ccl"))])
                    p.op("vector", lambda E: E.tensor_tensor(
                        out=WSB[:, 512:640], in0=WSB[:, 384:512].bitcast(F32),
                        in1=WSB[:, 0:128].bitcast(F32), op=ALU.mult),
                        sem="wcl", wait=[("eex", p.v("eex")), ("gcn", GCN),
                                         ("ymm", p.v("ymm"))])
                    p.op("vector", lambda E: E.tensor_tensor(
                        out=WSB[:, 768:896], in0=RT[:, 1152:1280],
                        in1=WSB[:, 640:768].bitcast(F32), op=ALU.mult), sem="ccl",
                        wait=[("ecl", p.v("ecl")), ("ymm", p.v("ymm"))])
                    spair = (sidx // 128) * 128
                    p.op("tensor", lambda E, spair=spair: (
                        E.matmul(PS[2][:, 0:128], WSB[:, 256:384],
                                 WSB[:, 512:640], start=True, stop=False),
                        E.matmul(PS[2][:, 0:128], states[:, spair:spair + 128],
                                 WSB[:, 768:896], start=False, stop=True))[-1],
                        sem="ymm", wait=[("wcl", p.v("wcl")), ("ccl", p.v("ccl")),
                                         ("xtv", XTV), ("yev", p.v("yev")),
                                         ("dsu", p.v("dsu"))])
                    p.op("vector", lambda E, xch=xch, tc=tc, hl=hl, h=h:
                         E.scalar_tensor_tensor(
                             out=xch[hl:hl + 64, tc * 128:(tc + 1) * 128],
                             in0=xch[hl:hl + 64, tc * 128:(tc + 1) * 128].bitcast(F32),
                             scalar=DBv[hl:hl + 64, h:h + 1],
                             in1=PS[2][hl:hl + 64, 0:128],
                             op0=ALU.mult, op1=ALU.add), sem="yev",
                         wait=[("ymm", p.v("ymm")), ("dbc", DBC)])
                    p.op("scalar", lambda E, col=col: (
                        E.activation(out=WSB[:, 896:897],
                                     in_=nST[:, col:col + 1].bitcast(F32R),
                                     func=ACTF.Exp, bias=SLB[:, col:col + 1]),
                        E.activation(out=WSB[:, 897:898],
                                     in_=SLB[:, col:col + 1].bitcast(F32R),
                                     func=ACTF.Exp))[-1], sem="exc",
                        wait=[("slc", SLC), ("dsu", p.v("dsu")),
                              ("bpp", p.v("bpp")), ("nst", NSTV)])
                    p.op("vector", lambda E: E.tensor_scalar(
                        out=WSB[:, 384:512], in0=WSB[:, 128:256].bitcast(F32),
                        scalar1=WSB[:, 896:897].bitcast(F32), scalar2=None,
                        op0=ALU.mult), sem="bpp",
                        wait=[("exc", p.v("exc")), ("wcl", p.v("wcl"))])
                    p.op("tensor", lambda E, h=h: E.matmul(
                        PS[3][:, 0:64], WSB[:, 384:512],
                        WSB[:, 256 + (h % 2) * 64:256 + (h % 2) * 64 + 64],
                        start=True, stop=True), sem="dmm",
                        wait=[("bpp", p.v("bpp")), ("dsu", p.v("dsu"))])
                    p.op("vector", lambda E, sidx=sidx: E.scalar_tensor_tensor(
                        out=states[:, sidx:sidx + 64],
                        in0=states[:, sidx:sidx + 64].bitcast(F32),
                        scalar=WSB[:, 897:898].bitcast(F32), in1=PS[3][:, 0:64],
                        op0=ALU.mult, op1=ALU.add), sem="dsu",
                        wait=[("dmm", p.v("dmm")), ("exc", p.v("exc"))])
            p.dma("sync", stb.ap()[0:NS, ps_ * 1024:(ps_ + 1) * 1024],
                  states[:].bitcast(F32), sem="std", wait=[("dsu", p.v("dsu"))])
        STD = p.v("std")

        agst = p.collective("AllGather", ALU.bypass, [stb.ap()], [stag.ap()],
                            sem="cc", wait=[("std", STD), ("lgd", p.v("lgd"))])

        p.dma("scalar", WSB[0:1, 0:512].rearrange("a (j c) -> a j c", c=64),
              stag.ap().rearrange("(j r) c -> j r c", r=NS + 1)[:, NS, 0:64]
              .rearrange("j c -> () j c").bitcast(F32R), sem="lgl",
              wait=[("cc", agst), ("dsu", p.v("dsu")), ("bpp", p.v("bpp"))])
        wrow = WSB[0:1, 0:512]
        prow = WSB[0:1, 512:1024]

        p.op("vector", lambda E: E.tensor_copy(out=prow[0:1, 0:64],
                                               in_=wrow[0:1, 0:64]),
             sem="wms", wait=[("lgl", p.v("lgl"))])
        for j in range(1, 8):
            p.op("vector", lambda E, j=j: E.tensor_tensor(
                out=prow[0:1, j * 64:(j + 1) * 64],
                in0=prow[0:1, (j - 1) * 64:j * 64],
                in1=wrow[0:1, j * 64:(j + 1) * 64], op=ALU.add),
                sem="wms", wait=[("wms", p.v("wms"))])
        p.op("vector", lambda E: E.tensor_tensor(
            out=wrow.rearrange("a (j h) -> a j h", h=64),
            in0=wrow.rearrange("a (j h) -> a j h", h=64),
            in1=R1[0:1, 48:56].rearrange("a j -> a j ()").broadcast_to([1, 8, 64]),
            op=ALU.mult), sem="wms", wait=[("wms", p.v("wms"))])
        p.op("vector", lambda E: E.tensor_reduce(
            out=BCT[0:1, 1536:1600],
            in_=wrow.rearrange("a (j h) -> a h j", h=64), axis=AXX,
            op=ALU.add), sem="wms", wait=[("wms", p.v("wms"))])
        p.op("vector", lambda E: E.tensor_tensor(
            out=prow.rearrange("a (j h) -> a j h", h=64),
            in0=BCT[0:1, 1536:1600].rearrange("a h -> a () h").broadcast_to([1, 8, 64]),
            in1=prow.rearrange("a (j h) -> a j h", h=64),
            op=ALU.subtract), sem="wms", wait=[("wms", p.v("wms"))])
        p.op("vector", lambda E: E.tensor_tensor(
            out=prow.rearrange("a (j h) -> a j h", h=64),
            in0=prow.rearrange("a (j h) -> a j h", h=64),
            in1=R1[0:1, 48:56].rearrange("a j -> a j ()").broadcast_to([1, 8, 64]),
            op=ALU.mult), sem="wmv", wait=[("wms", p.v("wms"))])
        p.op("scalar", lambda E: E.activation(out=prow, in_=prow, func=ACTF.Exp),
             sem="wme", wait=[("wmv", p.v("wmv"))])
        p.op("vector", lambda E: E.tensor_tensor(
            out=prow.rearrange("a (j h) -> a j h", h=64),
            in0=prow.rearrange("a (j h) -> a j h", h=64),
            in1=R1[0:1, 48:56].rearrange("a j -> a j ()").broadcast_to([1, 8, 64]),
            op=ALU.mult),
            sem="wmm", wait=[("wme", p.v("wme"))])
        p.op("tensor", lambda E: mmrow(E, PS[4], prow, 0, 512), sem="wbb",
             wait=[("wmm", p.v("wmm")), ("slb", p.v("slb"))])
        p.op("vector", lambda E: E.tensor_copy(out=BCT[:, 1536:2048],
                                               in_=PS[4][:, :]), sem="wbc",
             wait=[("wbb", p.v("wbb"))])
        WBC = p.v("wbc")
        WB = BCT[:, 1536:2048]

        stagr = stag.ap().rearrange("(j r) c -> j r c", r=NS + 1)
        for g in range(NG):
            for hh in range(8):
                h = g * 8 + hh
                for j_ in range(NCORES):
                    p.dma("sync",
                          EX[:, 1024 + j_ * 64:1024 + (j_ + 1) * 64],
                          stagr[j_, 0:NS, h * 64:(h + 1) * 64].bitcast(F32R),
                          sem="cgl", wait=([("cgm", p.v("cgm"))] if j_ == 0 else [])
                          + [("cc", agst)])

                for j in range(NCORES):
                    sl = EX[:, 1024 + j * 64:1024 + (j + 1) * 64].bitcast(F32)
                    if j == 0:
                        p.op("vector", lambda E, sl=sl, hh=hh, h=h: E.tensor_scalar(
                            out=WSB[:, hh * 64:(hh + 1) * 64], in0=sl,
                            scalar1=WB[:, h:h + 1],
                            scalar2=None, op0=ALU.mult), sem="cgs",
                            wait=[("cgl", p.v("cgl")), ("wbc", WBC)])
                    else:
                        p.op("vector", lambda E, sl=sl, hh=hh, h=h, j=j:
                             E.scalar_tensor_tensor(
                                 out=WSB[:, hh * 64:(hh + 1) * 64], in0=sl,
                                 scalar=WB[:, j * 64 + h:j * 64 + h + 1],
                                 in1=WSB[:, hh * 64:(hh + 1) * 64].bitcast(F32),
                                 op0=ALU.mult, op1=ALU.add),
                             sem="cgs" if j < NCORES - 1 else "cgm",
                             wait=[("cgs", p.v("cgs"))])
            CGM = p.v("cgm")
            for tc in range(NTC):
                p.rdma("sync", RT[:, 1152:1280],
                       cspill.ap()[g, :, tc * 128:(tc + 1) * 128], sem="crl",
                       wait=[("crc", p.v("crc"))])
                CRL = p.v("crl")
                for hh in range(8):
                    h = g * 8 + hh
                    fc = h // 2
                    hl = (h % 2) * 64
                    xch = x_of(fc)
                    col = tc * 64 + h
                    p.op("vector", lambda E, col=col: E.tensor_scalar(
                        out=WSB[:, 512:640], in0=onesm.bitcast(F32),
                        scalar1=dtAT[:, col:col + 1], scalar2=None, op0=ALU.mult),
                        sem="cw1", wait=[("csb", p.v("csb"))])
                    p.op("tensor", lambda E: E.matmul(
                        PS[1][:, 0:128], WSB[:, 512:640], tri,
                        start=True, stop=True), sem="csb",
                        wait=[("cw1", p.v("cw1")), ("cse", p.v("cse"))])
                    p.op("scalar", lambda E, col=col: E.activation(
                        out=WSB[:, 640:768], in_=PS[1][:, 0:128], func=ACTF.Exp,
                        bias=RoffB[:, col:col + 1]), sem="cse",
                        wait=[("csb", p.v("csb")), ("crc", p.v("crc"))])
                    p.op("vector", lambda E: E.tensor_tensor(
                        out=WSB[:, 768:896], in0=RT[:, 1152:1280],
                        in1=WSB[:, 640:768].bitcast(F32), op=ALU.mult), sem="crc",
                        wait=[("cse", p.v("cse")), ("crl", CRL), ("cgm", CGM),
                              ("cmm", p.v("cmm"))])
                    p.op("tensor", lambda E, hh=hh: E.matmul(
                        PS[2][:, 0:128], WSB[:, (hh // 2) * 128:(hh // 2) * 128 + 128],
                        WSB[:, 768:896], start=True, stop=True), sem="cmm",
                        wait=[("crc", p.v("crc")), ("cad", p.v("cad"))])
                    p.op("vector", lambda E, xch=xch, tc=tc, hl=hl: E.tensor_tensor(
                        out=xch[hl:hl + 64, tc * 128:(tc + 1) * 128],
                        in0=xch[hl:hl + 64, tc * 128:(tc + 1) * 128].bitcast(F32),
                        in1=PS[2][hl:hl + 64, 0:128], op=ALU.add), sem="cad",
                        wait=[("cmm", p.v("cmm"))])
        CORR = p.v("cad")

        for fc in range(32):
            for g in range(4):
                i = fc * 4 + g
                slot = WSB[:, (i % 2) * 512:(i % 2) * 512 + 512]
                p.rdma("sync", slot, wz[i], sem="gzl",
                       wait=[("gzc", i - 1)] + ([("cad", CORR)] if i < 2 else []))
                lv = p.v("gzl")

                def mmz(E, g=g, slot=slot, fc=fc):
                    last = None
                    for j in range(4):
                        dc = g * 4 + j
                        for ti in range(NQT):
                            last = E.matmul(
                                PS[(fc % 2) * 2 + ti][:, :],
                                slot[:, j * 128:(j + 1) * 128],
                                XR[:, dc * SQ + ti * QT: dc * SQ + (ti + 1) * QT],
                                start=(dc == 0), stop=(dc == DC - 1))
                    return last
                p.op("tensor", mmz, sem="gzc",
                     wait=[("gzl", lv)] + ([("gsl", p.v("gsl") - 1)] if g == 0 else []))

            def gsl(E, fc=fc):
                E.activation(out=RT[:, 0:512].bitcast(F32R), in_=PS[(fc % 2) * 2][:, :],
                             func=ACTF.Silu)
                return E.activation(out=RT[:, 512:1024].bitcast(F32R),
                                    in_=PS[(fc % 2) * 2 + 1][:, :], func=ACTF.Silu)
            p.op("scalar", gsl, sem="gsl",
                 wait=[("gzc", p.v("gzc")), ("gmu", p.v("gmu"))])
            ych = x_of(fc)
            p.op("vector", lambda E, ych=ych: E.tensor_tensor(
                out=ych, in0=ych.bitcast(F32), in1=RT[:, 0:1024], op=ALU.mult),
                sem="gmu", wait=[("gsl", p.v("gsl")), ("cad", CORR)])
            p.op("scalar", lambda E, ych=ych: E.activation(
                out=EX[:, 0:1024], in_=ych, func=ACTF.Square), sem="gsq",
                wait=[("gmu", p.v("gmu"))])

            def gss(E, fc=fc):
                E.matmul(PS[6][0:1, :], OC, EX[:, 0:512], start=(fc == 0),
                         stop=(fc == 31))
                return E.matmul(PS[7][0:1, :], OC, EX[:, 512:1024], start=(fc == 0),
                                stop=(fc == 31))
            p.op("tensor", gss, sem="gss", wait=[("gsq", p.v("gsq"))])
        GMU = p.v("gmu")

        def rsr(E):
            srow = BCT[0:1, 1024:2048]
            E.tensor_copy(out=srow[0:1, 0:512], in_=PS[6][0:1, :])
            return E.tensor_copy(out=srow[0:1, 512:1024], in_=PS[7][0:1, :])
        p.op("vector", rsr, sem="rs1", wait=[("gss", p.v("gss"))])
        p.op("scalar", lambda E: E.activation(
            out=BCT[0:1, 1024:2048], in_=BCT[0:1, 1024:2048], func=ACTF.Sqrt,
            bias=EPC[0:1, 0:1], scale=1.0 / MI),
            sem="rs2a", wait=[("rs1", p.v("rs1"))])

        def rs2f(E):
            E.reciprocal(out=BCT[0:1, 1024:2048], in_=BCT[0:1, 1024:2048])
            return E.tensor_copy(out=RT[0:1, 0:1024], in_=BCT[0:1, 1024:2048])
        p.op("vector", rs2f, sem="rs2", wait=[("rs2a", p.v("rs2a"))])

        def rsb(E):
            mmrow(E, PS[4], RT, 0, 512)
            return mmrow(E, PS[5], RT, 512, 512)
        p.op("tensor", rsb, sem="rs3", wait=[("rs2", p.v("rs2"))])

        def rsc(E):
            E.tensor_copy(out=BCT[:, 0:512], in_=PS[4][:, :])
            return E.tensor_copy(out=BCT[:, 512:1024], in_=PS[5][:, :])
        p.op("vector", rsc, sem="rs4", wait=[("rs3", p.v("rs3"))])
        RS4 = p.v("rs4")
        rstdB = BCT[:, 0:1024]
        p.dma("scalar", XR[:], xspill.ap().bitcast(F32R), sem="xrl",
              wait=[("gmu", GMU), ("dtm", p.v("dtm"))])
        XRL = p.v("xrl")

        for dc in range(32):
            ych = x_of(dc)
            p.op("vector", lambda E, ych=ych, dc=dc: E.scalar_tensor_tensor(
                out=ych, in0=ych.bitcast(F32),
                scalar=SM[:, SM_NRMW + dc:SM_NRMW + dc + 1], in1=rstdB,
                op0=ALU.mult, op1=ALU.mult), sem="gnm",
                wait=[("rs4", RS4)] if dc == 0 else ())
        GNM = p.v("gnm")
        for fc in range(DC):
            for g in range(8):
                i = fc * 8 + g
                slot = WSB[:, (i % 2) * 512:(i % 2) * 512 + 512]
                p.rdma("sync", slot, wom[i], sem="wol",
                       wait=[("woc", i - 1)] + ([("gzc", 128)] if i < 2 else []))
                lv = p.v("wol")

                def mmw(E, g=g, slot=slot, fc=fc):
                    last = None
                    for j in range(4):
                        dc = g * 4 + j
                        ych = x_of(dc)
                        for ti in range(NQT):
                            last = E.matmul(
                                PS[(fc % 2) * 2 + ti][:, :],
                                slot[:, j * 128:(j + 1) * 128],
                                ych[:, ti * QT:(ti + 1) * QT],
                                start=(dc == 0), stop=(dc == 31))
                    return last
                p.op("tensor", mmw, sem="woc",
                     wait=[("wol", lv), ("gnm", GNM)] +
                          ([("m9r", p.v("m9r") - 1)] if g == 0 else []))

            def madd(E, fc=fc):
                xc = XR[:, fc * SQ:(fc + 1) * SQ]
                E.tensor_tensor(out=xc[:, 0:512], in0=xc[:, 0:512].bitcast(F32),
                                in1=PS[(fc % 2) * 2][:, :], op=ALU.add)
                return E.tensor_tensor(out=xc[:, 512:1024],
                                       in0=xc[:, 512:1024].bitcast(F32),
                                       in1=PS[(fc % 2) * 2 + 1][:, :], op=ALU.add)
            p.op("vector", madd, sem="m9r", wait=[("woc", p.v("woc")), ("xrl", XRL)])
        MAMBA_DONE = p.v("m9r")

    if N_SUBLAYERS >= 4:
        FFN2_DONE = phase_ffn("m9r", MAMBA_DONE)
        final = ("f1ep", FFN2_DONE)
    elif N_SUBLAYERS == 3:
        final = ("m9r", MAMBA_DONE)
    elif N_SUBLAYERS == 2:
        final = ("f0ep", FFN1_DONE)
    else:
        final = ("orr", ATT_DONE)

    for c in range(DC):
        p.dma("sync", yout[c * 128:(c + 1) * 128, :],
              XR[:, c * SQ:(c + 1) * SQ].bitcast(F32), sem="yo",
              wait=[final] if c == 0 else ())
    p.wait("gpsimd", [("yo", p.v("yo"))] +
           ([("dbg", p.v("dbg"))] if p.v("dbg") else []))
    if p.v("dbg2x"):
        p.wait("sync", [("dbg2x", p.v("dbg2x")), ("dbgq", p.v("dbgq"))])

    with nc.Block() as block:
        @block.sync
        def _(E):
            for fn in p.q["sync"]:
                fn(E)

        @block.gpsimd
        def _(E):
            for fn in p.q["gpsimd"]:
                fn(E)

        @block.scalar
        def _(E):
            for fn in p.q["scalar"]:
                fn(E)

        @block.vector
        def _(E):
            for fn in p.q["vector"]:
                fn(E)

        @block.tensor
        def _(E):
            for fn in p.q["tensor"]:
                fn(E)
    return nc


def _pieces_st(w, n_in, n_out):
    nic, noc = n_in // 128, n_out // 128
    a = np.asarray(w, dtype=np.float32).reshape(nic, 128, noc, 128).transpose(2, 0, 1, 3)
    a = a.reshape(noc, nic // 4, 4, 128, 128).transpose(0, 1, 3, 2, 4)
    return np.ascontiguousarray(a.reshape(noc * (nic // 4), 128, 512))


def _rope_perm():
    per = np.concatenate([np.arange(0, 64, 2), np.arange(1, 64, 2),
                          np.arange(64, 128)])
    return np.concatenate([h * 128 + per for h in range(NH)])


def kernel(**inputs):
    X = np.asarray(inputs["L0_attn_a_X"])[0]
    W_in = np.asarray(inputs["L0_attn_a_W_in"])
    perm = _rope_perm()
    wq_p = _pieces_st(W_in[:, 0:D][:, perm], D, D)
    wk_p = _pieces_st(W_in[:, D:2 * D][:, perm], D, D)
    Wv = np.asarray(W_in[:, 2 * D:3 * D], dtype=np.float32)
    wv_h = np.ascontiguousarray(
        Wv.reshape(DC, 128, 2, 1024).transpose(2, 0, 1, 3))
    wo_p = _pieces_st(np.asarray(inputs["L0_attn_a_W_out"]), D, D)
    ffnw_p = _pieces_st(np.ascontiguousarray(np.asarray(inputs["L0_ffn_weight"]).T),
                        D, D)
    wz_p = _pieces_st(np.asarray(inputs["L0_attn_b_Wz"]), D, MI)
    wx_p = _pieces_st(np.asarray(inputs["L0_attn_b_Wx"]), D, MI)
    wb_p = _pieces_st(np.asarray(inputs["L0_attn_b_WB"]), D, GN)
    wc_p = _pieces_st(np.asarray(inputs["L0_attn_b_WC"]), D, GN)
    Wdt = np.asarray(inputs["L0_attn_b_Wdt"], dtype=np.float32).reshape(DC, 128, 64)
    wdt_p = np.ascontiguousarray(
        Wdt.reshape(2, 8, 128, 64).transpose(0, 2, 1, 3).reshape(2, 128, 512))
    wom_p = _pieces_st(np.asarray(inputs["L0_attn_b_Wout"]), MI, D)

    def colchunks(v, n):
        return np.ascontiguousarray(np.asarray(v, dtype=np.float32).reshape(n, 128).T)

    ln_w_h = colchunks(inputs["L0_attn_a_ln_w"], DC)
    ln_b_h = colchunks(inputs["L0_attn_a_ln_b"], DC)
    ffnb_h = colchunks(inputs["L0_ffn_bias"], DC)
    normw_h = colchunks(inputs["L0_attn_b_norm_w"], 32)
    convb_h = colchunks(inputs["L0_attn_b_conv_b"], NCC)
    convw_h = np.ascontiguousarray(
        np.asarray(inputs["L0_attn_b_conv_w"], dtype=np.float32)
        .reshape(NCC, 128, 4).transpose(1, 0, 2))
    dtb_h = np.asarray(inputs["L0_attn_b_dt_bias"], np.float32).reshape(64, 1)
    alog_h = np.asarray(inputs["L0_attn_b_A_log"], np.float32).reshape(64, 1)
    dvec_h = np.asarray(inputs["L0_attn_b_D"], np.float32).reshape(64, 1)

    identity = np.eye(128, dtype=np.float32)
    trim = (np.arange(128)[:, None] <= np.arange(128)[None, :]).astype(np.float32)
    onesm_h = np.ones((128, 128), np.float32)
    onesc_h = np.ones((128, 1), np.float32)

    inv_freq = (1.0 / (10000.0 ** (np.arange(0, 64, 2, dtype=np.float32) / 64.0))
                ).astype(np.float32)

    in_maps = []
    for i in range(NCORES):
        pos = np.arange(i * SQ, (i + 1) * SQ, dtype=np.float32)
        ang = (pos[:, None] * inv_freq[None, :]).astype(np.float32)
        cosj = np.cos(ang).astype(np.float32).T
        sinj = np.sin(ang).astype(np.float32).T
        cos2 = np.ascontiguousarray(np.concatenate([cosj, cosj], 0))
        sinS = np.ascontiguousarray(np.concatenate([-sinj, sinj], 0))
        pm = np.zeros((1, 8), np.float32)
        pm[0, :i] = 1.0
        hs = np.zeros((1, 8), np.float32)
        if i > 0:
            hs[0, i - 1] = 1.0
        in_maps.append({
            "xT": np.ascontiguousarray(X[i * SQ:(i + 1) * SQ].T, dtype=np.float32),
            "wq": wq_p, "wk": wk_p, "wv": wv_h, "wo": wo_p,
            "ln_w": ln_w_h, "ln_b": ln_b_h, "ffnw": ffnw_p, "ffnb": ffnb_h,
            "wz": wz_p, "wxp": wx_p, "wbp": wb_p, "wcp": wc_p, "wdt": wdt_p,
            "wom": wom_p, "convw": convw_h, "convb": convb_h,
            "dtbias": dtb_h, "alog": alog_h, "dvec": dvec_h, "normw": normw_h,
            "ropec": cos2, "ropes": sinS, "prevmask": pm, "halosel": hs,
            "identc": identity, "trimask": trim, "onesmat": onesm_h,
            "onescol": onesc_h, "epscol": np.full((128, 1), EPS, np.float32),
            "zrow": np.zeros((1, MH * MD), np.float32),
            "haloselB": np.tile(hs, (128, 1)),
        })

    nc = build_program()
    res = run_bass_kernel_spmd(nc, in_maps, core_ids=list(range(NCORES)))
    out = np.empty((1, S, D), dtype=np.float32)
    for i in range(NCORES):
        out[0, i * SQ:(i + 1) * SQ] = res.results[i]["y"].T
    return out
